# version 11
# baseline (speedup 1.0000x reference)
import numpy as np
import concourse.bass as bass
from concourse.bass import IndirectOffsetOnAxis
import concourse.mybir as mybir
from concourse.bass_utils import run_bass_kernel_spmd
from contextlib import ExitStack

F32 = mybir.dt.float32
BF16 = mybir.dt.bfloat16
AF = mybir.ActivationFunctionType
ALU = mybir.AluOpType
AX = mybir.AxisListType

B, T, D, H, DH, E = 2, 4096, 2048, 16, 128, 8
NCH = 16
NT = 32
NSTR = 8
NG = 8
K_SLOT = [(32 - 4 * j) * 128 for j in range(8)]
EPS = 1e-5
CAPT = 48
CAPE = 8 * CAPT


def own_tiles_for(q):
    return [31 - (4 * j + (q if j % 2 == 0 else 3 - q)) for j in range(8)]



class Sem:
    def __init__(self, handle, name):
        self.h, self.name, self.n = handle, name, 0


class Rec:
    __slots__ = ("eng", "fn", "waits", "incs", "after")

    def __init__(self, eng, fn, waits, incs):
        self.eng, self.fn, self.waits, self.incs = eng, fn, list(waits), list(incs)
        self.after = None

    def tok(self):
        assert self.after is not None
        return self.after


class Prog:
    def __init__(self):
        self.ops = []
        self.last_wait = {}

    def op(self, eng, fn, waits=(), incs=()):
        wl = []
        for w in waits:
            if w is None:
                continue
            if isinstance(w, Rec):
                w = w.tok()
            sem, cnt = w
            assert cnt <= sem.n, f"wait on future count {cnt} > {sem.n} ({sem.name})"
            wl.append((sem, cnt))
        r = Rec(eng, fn, wl, incs)
        for sem, k in incs:
            sem.n += k
            if r.after is None:
                r.after = (sem, sem.n)
        self.ops.append(r)
        return r

    def flush(self, block):
        ops, self.ops = self.ops, []
        by_eng = {}
        for r in ops:
            by_eng.setdefault(r.eng, []).append(r)
        for engname, recs in by_eng.items():
            def run(eng, recs=recs, engname=engname):
                for r in recs:
                    for sem, cnt in r.waits:
                        key = (engname, sem.name)
                        if self.last_wait.get(key, -1) < cnt:
                            eng.wait_ge(sem.h, cnt)
                            self.last_wait[key] = cnt
                    ins = r.fn(eng)
                    for sem, k in r.incs:
                        assert ins is not None
                        ins.then_inc(sem.h, k)
            getattr(block, engname)(run)



def build_nc(debug=False):
    nc = bass.Bass("TRN2", target_bir_lowering=False)
    dbg = {}
    if debug:
        dbg["xn"] = nc.dram_tensor("dbg_xn", [16, 128, 1024], F32, kind="ExternalOutput").ap()
        dbg["fm0"] = nc.dram_tensor("dbg_fm0", [16, 128, 512], F32, kind="ExternalOutput").ap()
        dbg["kf"] = nc.dram_tensor("dbg_kf", [128, 8192], F32, kind="ExternalOutput").ap()
        dbg["vt"] = nc.dram_tensor("dbg_vt", [128, 8192], F32, kind="ExternalOutput").ap()
        dbg["qf"] = nc.dram_tensor("dbg_qf", [128, 2048], F32, kind="ExternalOutput").ap()
        dbg["sc"] = nc.dram_tensor("dbg_sc", [128, 4096], F32, kind="ExternalOutput").ap()
        dbg["pr"] = nc.dram_tensor("dbg_pr", [128, 4096], F32, kind="ExternalOutput").ap()
        dbg["hd"] = nc.dram_tensor("dbg_hd", [128, 16384], F32, kind="ExternalOutput").ap()
        dbg["ac1"] = nc.dram_tensor("dbg_ac1", [128, 16384], F32, kind="ExternalOutput").ap()
        dbg["ac2"] = nc.dram_tensor("dbg_ac2", [128, 16384], F32, kind="ExternalOutput").ap()
        dbg["gmt"] = nc.dram_tensor("dbg_gmt", [8, 1024], F32, kind="ExternalOutput").ap()
        dbg["xe"] = nc.dram_tensor("dbg_xe", [128, 16384], F32, kind="ExternalOutput").ap()

    d_xseq = nc.dram_tensor("xseq", [T, D], F32, kind="ExternalInput").ap()
    d_xown = nc.dram_tensor("xown", [8 * 128, D], F32, kind="ExternalInput").ap()
    d_cosk = nc.dram_tensor("cosk", [128, T], F32, kind="ExternalInput").ap()
    d_sink = nc.dram_tensor("sink", [128, T], F32, kind="ExternalInput").ap()
    d_cosq = nc.dram_tensor("cosq", [128, 1024], F32, kind="ExternalInput").ap()
    d_sinq = nc.dram_tensor("sinq", [128, 1024], F32, kind="ExternalInput").ap()
    d_mask = nc.dram_tensor("maskt", [8, 128, 512], F32, kind="ExternalInput").ap()
    d_pswp = nc.dram_tensor("pswp", [128, 128], F32, kind="ExternalInput").ap()
    d_sel = nc.dram_tensor("sel8", [8, 1024], F32, kind="ExternalInput").ap()
    d_wq = nc.dram_tensor("wq_p", [16, 128, 2048], F32, kind="ExternalInput").ap()
    d_wk = nc.dram_tensor("wk_p", [16, 128, 2048], F32, kind="ExternalInput").ap()
    d_wv = nc.dram_tensor("wv_n", [2048, 2048], F32, kind="ExternalInput").ap()
    d_wo = nc.dram_tensor("wo_p", [16, 128, 2048], F32, kind="ExternalInput").ap()
    d_rw = nc.dram_tensor("rw_n", [2048, 8], F32, kind="ExternalInput").ap()
    d_g = nc.dram_tensor("gate_p", [8, 16, 128, 2048], F32, kind="ExternalInput").ap()
    d_u = nc.dram_tensor("up_p", [8, 16, 128, 2048], F32, kind="ExternalInput").ap()
    d_dn = nc.dram_tensor("down_p", [8, 16, 128, 2048], F32, kind="ExternalInput").ap()
    d_tril = nc.dram_tensor("tril_t", [128, 128], F32, kind="ExternalInput").ap()
    d_evals = nc.dram_tensor("evals_t", [128, 8], F32, kind="ExternalInput").ap()
    d_sg = nc.dram_tensor("sg_p", [16, 128, 2048], F32, kind="ExternalInput").ap()
    d_su = nc.dram_tensor("su_p", [16, 128, 2048], F32, kind="ExternalInput").ap()
    d_sd = nc.dram_tensor("sd_p", [16, 128, 2048], F32, kind="ExternalInput").ap()
    d_out = nc.dram_tensor("out_o", [8, 128, 2048], F32, kind="ExternalOutput").ap()

    fm_scr = nc.dram_tensor("fm_scr", [NSTR, NCH, 128, 512], BF16, kind="Internal").ap()
    own_scr = nc.dram_tensor("own_scr", [NCH, 128, 1024], BF16, kind="Internal").ap()
    own_tm = nc.dram_tensor("own_tm", [1024, 2048], BF16, kind="Internal").ap()
    cpq_scr = nc.dram_tensor("cpq_scr", [8 * CAPE, 2048], BF16, kind="Internal").ap()
    z_scr = nc.dram_tensor("z_scr", [8 * CAPE, 2048], BF16, kind="Internal").ap()
    idx_scr = nc.dram_tensor("idx_scr", [128, 8], mybir.dt.int32, kind="Internal").ap()

    P = Prog()

    with ExitStack() as top:
        EN = top.enter_context
        blk = EN(nc.Block())

        sq = Sem(EN(nc.semaphore("sq")), "sq")
        gq = Sem(EN(nc.semaphore("gq")), "gq")
        vt = Sem(EN(nc.semaphore("vt")), "vt")
        vv = Sem(EN(nc.semaphore("vv")), "vv")
        vs = Sem(EN(nc.semaphore("vs")), "vs")
        vg = Sem(EN(nc.semaphore("vg")), "vg")

        idn_b = EN(nc.sbuf_tensor("idn_b", [128, 128], BF16))
        idn_f = EN(nc.sbuf_tensor("idn_f", [128, 128], F32))
        heads = EN(nc.sbuf_tensor("heads", [128, 16 * 1024], BF16))

        def ld(dst_ap, src_ap):
            return lambda e: e.dma_start(out=dst_ap, in_=src_ap)

        def mk_idn(idn):
            def f(e):
                e.memset(idn[:, :], 0.0)
                return e.affine_select(out=idn[:, :], in_=idn[:, :],
                                       compare_op=ALU.not_equal, fill=1.0,
                                       base=0, pattern=[[-1, 128]],
                                       channel_multiplier=1)
            return f
        r_idn = P.op("gpsimd", mk_idn(idn_b), incs=[(vg, 1)])
        r_idnf = P.op("gpsimd", mk_idn(idn_f), incs=[(vg, 1)])

        def emit_ln(pfx, src, tiles, strip_cols, dst_view, scope, first_waits,
                    tm_dst=None):
            EL = scope.enter_context
            xb = [EL(nc.sbuf_tensor(f"{pfx}xb{i}", [128, D], F32)) for i in range(2)]
            xnb = [EL(nc.sbuf_tensor(f"{pfx}xnb{i}", [128, D], BF16)) for i in range(2)]
            junk = EL(nc.sbuf_tensor(f"{pfx}junk", [128, D], F32))
            stat = [EL(nc.sbuf_tensor(f"{pfx}st{i}", [128, 16], F32)) for i in range(2)]
            strip = EL(nc.sbuf_tensor(f"{pfx}strip", [128, 16 * strip_cols], BF16))
            pT = [EL(nc.psum_tensor(f"{pfx}pT{i}", [128, 1024], BF16)) for i in range(4)]
            per_strip = strip_cols // 128
            r_junk_free = [None]
            r_read = [None, None]
            r_tm = [None, None]
            r_tr = [None, None, None, None]
            r_ev = [None, None]
            r_wr = None
            recs = {}
            fw = list(first_waits)
            for it, t in enumerate(tiles):
                par = it % 2
                rx = P.op("sync", ld(xb[par][:, :], src[t * 128:(t + 1) * 128, :]),
                          waits=[r_read[par]], incs=[(sq, 16)])
                s1 = stat[par][:, 0:1]; s2 = stat[par][:, 1:2]
                mean = stat[par][:, 2:3]; em2 = stat[par][:, 3:4]
                msq = stat[par][:, 4:5]; var = stat[par][:, 5:6]
                sd_ = stat[par][:, 6:7]; rstd = stat[par][:, 7:8]
                nmr = stat[par][:, 8:9]

                def f_sum(e, par=par, s1=s1):
                    return e.reduce_sum(s1, xb[par][:, :], axis=AX.X)
                P.op("vector", f_sum, waits=[rx], incs=[(vv, 1)])

                def f_sq(e, par=par):
                    return e.activation(junk[:, :], xb[par][:, :], AF.Square)
                rs1 = P.op("scalar", f_sq, waits=[rx, r_junk_free[0]],
                           incs=[(vs, 1)])

                def f_sqsum(e, s2=s2):
                    return e.reduce_sum(s2, junk[:, :], axis=AX.X)
                rs1 = P.op("vector", f_sqsum, waits=[rs1], incs=[(vv, 1)])
                r_junk_free[0] = rs1

                def f_st1(e, s1=s1, s2=s2, mean=mean, em2=em2):
                    e.tensor_scalar_mul(mean, s1, 1.0 / D)
                    return e.tensor_scalar(em2, s2, 1.0 / D, EPS,
                                           ALU.mult, ALU.add)
                rv2a = P.op("vector", f_st1, waits=[rs1], incs=[(vv, 1)])

                def f_st2(e, mean=mean, msq=msq):
                    return e.tensor_mul(msq, mean, mean)
                rv2b = P.op("vector", f_st2, waits=[rv2a], incs=[(vv, 1)])

                def f_st3(e, msq=msq, em2=em2, var=var):
                    return e.scalar_tensor_tensor(var, msq, -1.0, em2,
                                                  ALU.mult, ALU.add)
                rv2 = P.op("vector", f_st3, waits=[rv2b], incs=[(vv, 1)])

                def f_sd(e, var=var, sd_=sd_):
                    return e.activation(sd_, var, AF.Sqrt)
                rs2 = P.op("scalar", f_sd, waits=[rv2], incs=[(vs, 1)])

                def f_recip(e, sd_=sd_, rstd=rstd):
                    return e.reciprocal(rstd, sd_)
                rv3a = P.op("vector", f_recip, waits=[rs2], incs=[(vv, 1)])

                def f_nmr(e, rstd=rstd, mean=mean, nmr=nmr):
                    return e.scalar_tensor_tensor(nmr, mean, -1.0, rstd,
                                                  ALU.mult, ALU.mult)
                rv3 = P.op("vector", f_nmr, waits=[rv3a], incs=[(vv, 1)])

                def f_xn(e, par=par, rstd=rstd, nmr=nmr):
                    return e.tensor_scalar(xnb[par][:, :], xb[par][:, :],
                                           rstd, nmr, ALU.mult, ALU.add)
                rs3 = P.op("vector", f_xn,
                           waits=[rv3, r_tr[par * 2], r_tr[par * 2 + 1],
                                  r_tm[par]],
                           incs=[(vv, 1)])
                r_read[par] = rs3

                if tm_dst is not None:
                    def f_tm(e, it=it, par=par):
                        return e.dma_start(out=tm_dst[it * 128:(it + 1) * 128, :],
                                           in_=xnb[par][:, :])
                    r_tm[par] = P.op("sync", f_tm, waits=[rs3], incs=[(sq, 16)])

                for half in range(2):
                    hidx = par * 2 + half

                    def f_tr(e, par=par, half=half, hidx=hidx):
                        ins = None
                        for c8 in range(8):
                            c = half * 8 + c8
                            ins = e.transpose(pT[hidx][:, c8 * 128:(c8 + 1) * 128],
                                              xnb[par][:, c * 128:(c + 1) * 128],
                                              idn_b[:, :])
                        return ins
                    r_tr[hidx] = P.op("tensor", f_tr,
                                      waits=[rs3, r_idn, r_ev[par]] + fw,
                                      incs=[(vt, 1)])
                    fw = []

                pos = it % per_strip

                def f_ev(e, par=par, pos=pos):
                    ins = None
                    for half in range(2):
                        hidx = par * 2 + half
                        dst = strip.ap().rearrange("p (c n) -> p c n", c=16)[
                            :, half * 8:(half + 1) * 8, pos * 128:(pos + 1) * 128]
                        ins = e.tensor_copy(dst, pT[hidx].ap().rearrange(
                            "p (c n) -> p c n", c=8))
                    return ins
                rev = P.op("vector", f_ev,
                           waits=[r_tr[par * 2], r_tr[par * 2 + 1], r_wr],
                           incs=[(vv, 1)])
                r_ev[par] = rev

                if pos == per_strip - 1:
                    sidx = it // per_strip

                    def f_wr(e, sidx=sidx):
                        return e.dma_start(out=dst_view(sidx),
                                           in_=strip.ap().rearrange(
                                               "p (c n) -> p c n", c=16))
                    r_wr = P.op("sync", f_wr, waits=[r_ev[0], r_ev[1]],
                                incs=[(sq, 16)])
                    recs[sidx] = r_wr
            return recs, rev

        with ExitStack() as p2s:
            E2 = p2s.enter_context
            cosk = E2(nc.sbuf_tensor("cosk_s", [128, T], BF16))
            sink = E2(nc.sbuf_tensor("sink_s", [128, T], BF16))
            cosq = E2(nc.sbuf_tensor("cosq_s", [128, 1024], BF16))
            sinq = E2(nc.sbuf_tensor("sinq_s", [128, 1024], BF16))
            maskt = E2(nc.sbuf_tensor("maskt_s", [128, 8 * 512], BF16))
            pswp = E2(nc.sbuf_tensor("pswp_s", [128, 128], BF16))
            P.op("gpsimd", ld(cosk[:, :], d_cosk), incs=[(gq, 16)])
            P.op("gpsimd", ld(sink[:, :], d_sink), incs=[(gq, 16)])
            P.op("gpsimd", ld(cosq[:, :], d_cosq), incs=[(gq, 16)])
            P.op("gpsimd", ld(sinq[:, :], d_sinq), incs=[(gq, 16)])
            P.op("gpsimd", ld(maskt.ap().rearrange("p (j n) -> p j n", j=8),
                              d_mask.rearrange("j p n -> p j n")), incs=[(gq, 16)])
            r_consts = P.op("gpsimd", ld(pswp[:, :], d_pswp), incs=[(gq, 16)])
            wkb = E2(nc.sbuf_tensor("wkb", [128, 2 * 2048], BF16))
            wvb = E2(nc.sbuf_tensor("wvb", [128, 16 * 256], BF16))
            xns = [E2(nc.sbuf_tensor(f"xns{i}", [128, 16 * 512], BF16)) for i in range(2)]
            Kf = E2(nc.sbuf_tensor("Kf", [128, 2 * 4096], BF16))
            Vt = E2(nc.sbuf_tensor("Vt", [128, 32 * 256], BF16))
            Qf = E2(nc.sbuf_tensor("Qf", [128, 2 * 1024], BF16))
            kpre = [E2(nc.sbuf_tensor(f"kpre{i}", [128, 512], BF16)) for i in range(2)]
            qpre = [E2(nc.sbuf_tensor(f"qpre{i}", [128, 1024], BF16)) for i in range(2)]
            t1b = E2(nc.sbuf_tensor("t1b", [128, 512], BF16))
            t2b = E2(nc.sbuf_tensor("t2b", [128, 512], BF16))
            with ExitStack() as p1s:
                fm_recs, r_p1_ev = emit_ln(
                    "a", d_xseq, list(range(NT)), 512,
                    lambda s: fm_scr[s].rearrange("c p n -> p c n"), p1s, [])
                P.flush(blk)
            with ExitStack() as p15s:
                own_recs, r_own_ev = emit_ln(
                    "b", d_xown, list(range(8)), 512,
                    lambda sx: own_scr.rearrange("c p n -> p c n")[
                        :, :, sx * 512:(sx + 1) * 512],
                    p15s, [r_p1_ev], tm_dst=own_tm)
                r_own_wr = own_recs[max(own_recs)]
                P.flush(blk)
            if debug:
                P.op("gpsimd", ld(dbg["xn"], own_scr), waits=[r_own_wr],
                     incs=[(gq, 16)])
                P.op("gpsimd", ld(dbg["fm0"], fm_scr[0]), waits=[fm_recs[0]],
                     incs=[(gq, 16)])

            scores = [E2(nc.sbuf_tensor(f"sc{i}", [128, 4096], F32)) for i in range(2)]
            probs = E2(nc.sbuf_tensor("probs", [128, 4096], BF16))
            ptb = E2(nc.sbuf_tensor("ptb", [128, 2048], BF16))
            smax = E2(nc.sbuf_tensor("smax", [128, 4], F32))

            r_w_consumer = None
            r_xns_cons = [None, None]
            r_sc_free = [None, None]
            r_pS_free = [None] * 4
            r_probs_free = None
            r_pO_free = [None, None]
            r_oev_last = None
            r_ptev_last = None
            r_exp_prev = None
            r_prev_attn_done = r_own_ev
            r_dbg_hold_prev = []

            for g in range(NG):
                with ExitStack() as gps:
                    E3 = gps.enter_context
                    pK = [[E3(nc.psum_tensor(f"pK{g}{h}{i}", [128, 512], F32))
                           for i in range(2)] for h in range(2)]
                    pV = E3(nc.psum_tensor(f"pV{g}", [128, 1024], F32))
                    pW = [E3(nc.psum_tensor(f"pW{g}{i}", [128, 512], F32))
                          for i in range(2)]

                    wdeps = [r_w_consumer, r_consts]
                    r_wk = P.op("gpsimd", ld(
                        wkb.ap().rearrange("p (h n) -> p h n", h=2),
                        d_wk[2 * g:2 * g + 2].rearrange("h p n -> p h n")),
                        waits=wdeps, incs=[(gq, 16)])
                    r_wv = P.op("gpsimd", ld(
                        wvb.ap().rearrange("p (c n) -> p c n", c=16),
                        d_wv[:, g * 256:(g + 1) * 256].rearrange(
                            "(c p) n -> p c n", p=128)),
                        waits=wdeps, incs=[(gq, 16)])

                    r_rope = [None] * NSTR
                    r_kpre = [None] * NSTR
                    r_vev = [None] * NSTR
                    r_kmm = [None] * NSTR
                    for s in range(NSTR):
                        par = s % 2
                        r_xn = P.op("sync", ld(
                            xns[par].ap().rearrange("p (c n) -> p c n", c=16),
                            fm_scr[s].rearrange("c p n -> p c n")),
                            waits=[fm_recs[s], r_xns_cons[par]], incs=[(sq, 16)])

                        def f_kmm(e, par=par, g=g):
                            ins = None
                            for c in range(16):
                                for h in range(2):
                                    ins = e.matmul(
                                        pK[h][par][:, :],
                                        wkb[:, h * 2048 + c * 128: h * 2048 + (c + 1) * 128],
                                        xns[par][:, c * 512:(c + 1) * 512],
                                        start=(c == 0), stop=(c == 15))
                            return ins
                        extra = ([r_prev_attn_done, r_oev_last, r_ptev_last]
                                 + r_dbg_hold_prev if s == 0 else [])
                        rkmm = P.op("tensor", f_kmm,
                                    waits=[r_xn, r_wk,
                                           r_kpre[s - 2] if s >= 2 else None] + extra,
                                    incs=[(vt, 1)])
                        r_kmm[s] = rkmm

                        def f_vmm(e, par=par, g=g):
                            ins = None
                            for sub in range(4):
                                for c in range(16):
                                    ins = e.matmul(
                                        pV[:, sub * 256:(sub + 1) * 256],
                                        xns[par][:, c * 512 + sub * 128: c * 512 + (sub + 1) * 128],
                                        wvb[:, c * 256:(c + 1) * 256],
                                        start=(c == 0), stop=(c == 15))
                            return ins
                        rvmm = P.op("tensor", f_vmm,
                                    waits=[r_xn, r_wv,
                                           r_vev[s - 1] if s >= 1 else None],
                                    incs=[(vt, 1)])
                        r_xns_cons[par] = rvmm

                        def f_kpre(e, par=par, g=g):
                            e.tensor_copy(kpre[0][:, :], pK[0][par][:, :])
                            return e.tensor_copy(kpre[1][:, :], pK[1][par][:, :])
                        rkp = P.op("vector", f_kpre, waits=[rkmm], incs=[(vv, 1)])
                        r_kpre[s] = rkp

                        def f_swp(e, g=g):
                            e.matmul(pW[0][:, :], pswp[:, :], kpre[0][:, :],
                                     start=True, stop=True)
                            return e.matmul(pW[1][:, :], pswp[:, :], kpre[1][:, :],
                                            start=True, stop=True)
                        rsw = P.op("tensor", f_swp,
                                   waits=[rkp, r_consts,
                                          r_rope[s - 1] if s >= 1 else None],
                                   incs=[(vt, 1)])

                        rro = [rsw, r_consts]
                        for h in range(2):
                            def f_rope1(e, g=g, s=s, h=h):
                                e.tensor_mul(t1b[:, :],
                                             cosk[:, s * 512:(s + 1) * 512],
                                             kpre[h][:, :])
                                return e.tensor_mul(
                                    t2b[:, :], sink[:, s * 512:(s + 1) * 512],
                                    pW[h][:, :])
                            ra = P.op("vector", f_rope1, waits=rro, incs=[(vv, 1)])

                            def f_rope2(e, g=g, s=s, h=h):
                                return e.tensor_add(
                                    Kf[:, h * 4096 + s * 512: h * 4096 + (s + 1) * 512],
                                    t1b[:, :], t2b[:, :])
                            rro = [P.op("vector", f_rope2, waits=[ra], incs=[(vv, 1)])]
                        r_rope[s] = rro[0]

                        def f_vev(e, s=s, g=g):
                            return e.tensor_copy(
                                Vt[:, (4 * s) * 256:(4 * s + 4) * 256], pV[:, :])
                        r_vev[s] = P.op("vector", f_vev, waits=[rvmm], incs=[(vv, 1)])

                    r_wq = P.op("gpsimd", ld(
                        wkb.ap().rearrange("p (h n) -> p h n", h=2),
                        d_wq[2 * g:2 * g + 2].rearrange("h p n -> p h n")),
                        waits=[r_kmm[NSTR - 1]], incs=[(gq, 16)])
                    r_qhalf = [None, None]
                    for hf in range(2):
                        r_qhalf[hf] = P.op("sync", ld(
                            xns[hf].ap().rearrange("p (c n) -> p c n", c=8),
                            own_scr[8 * hf:8 * hf + 8].rearrange("c p n -> p c n")),
                            waits=[r_own_wr, r_xns_cons[hf]], incs=[(sq, 16)])

                    def f_qmm(e, g=g):
                        ins = None
                        for hf in range(2):
                            for c8 in range(8):
                                c = 8 * hf + c8
                                for h in range(2):
                                    for th in range(2):
                                        ins = e.matmul(
                                            pK[h][th][:, :],
                                            wkb[:, h * 2048 + c * 128: h * 2048 + (c + 1) * 128],
                                            xns[hf][:, c8 * 1024 + th * 512: c8 * 1024 + (th + 1) * 512],
                                            start=(c == 0), stop=(c == 15))
                        return ins
                    rqmm = P.op("tensor", f_qmm,
                                waits=[r_qhalf[0], r_qhalf[1], r_wq,
                                       r_kpre[NSTR - 1]],
                                incs=[(vt, 1)])
                    r_xns_cons = [rqmm, rqmm]
                    r_w_consumer = rqmm

                    def f_qpre(e, g=g):
                        ins = None
                        for h in range(2):
                            for th in range(2):
                                ins = e.tensor_copy(
                                    qpre[h][:, th * 512:(th + 1) * 512],
                                    pK[h][th][:, :])
                        return ins
                    rqp = P.op("vector", f_qpre, waits=[rqmm], incs=[(vv, 1)])

                    r_qrope = None
                    for h in range(2):
                        def f_qsw(e, h=h, g=g):
                            e.matmul(pW[0][:, :], pswp[:, :], qpre[h][:, 0:512],
                                     start=True, stop=True)
                            return e.matmul(pW[1][:, :], pswp[:, :],
                                            qpre[h][:, 512:1024],
                                            start=True, stop=True)
                        rqs = P.op("tensor", f_qsw,
                                   waits=[rqp, r_rope[NSTR - 1], r_qrope],
                                   incs=[(vt, 1)])

                        rqro = [rqs]
                        for th in range(2):
                            def f_qro1(e, h=h, g=g, th=th):
                                e.tensor_mul(t1b[:, :],
                                             cosq[:, th * 512:(th + 1) * 512],
                                             qpre[h][:, th * 512:(th + 1) * 512])
                                return e.tensor_mul(
                                    t2b[:, :], sinq[:, th * 512:(th + 1) * 512],
                                    pW[th][:, :])
                            ra = P.op("vector", f_qro1, waits=rqro, incs=[(vv, 1)])

                            def f_qro2(e, h=h, g=g, th=th):
                                return e.tensor_add(
                                    Qf[:, h * 1024 + th * 512: h * 1024 + (th + 1) * 512],
                                    t1b[:, :], t2b[:, :])
                            rqro = [P.op("vector", f_qro2, waits=[ra], incs=[(vv, 1)])]
                        r_qrope = rqro[0]

                if debug and g == 7:
                    rd1 = P.op("gpsimd", ld(dbg["kf"], Kf[:, :]),
                               waits=[r_rope[NSTR - 1]], incs=[(gq, 16)])
                    rd2 = P.op("gpsimd", ld(dbg["vt"], Vt[:, :]),
                               waits=[r_vev[NSTR - 1]], incs=[(gq, 16)])
                    rd3 = P.op("gpsimd", ld(dbg["qf"], Qf[:, :]),
                               waits=[r_qrope], incs=[(gq, 16)])
                    r_qrope = rd3
                    r_dbg_hold = [rd1, rd2]
                else:
                    r_dbg_hold = []

                with ExitStack() as ats:
                    E4 = ats.enter_context
                    pS = [E4(nc.psum_tensor(f"pS{g}{i}", [128, 512], F32))
                          for i in range(4)]
                    pTt = [E4(nc.psum_tensor(f"pTt{g}{i}", [128, 1024], BF16))
                           for i in range(2)]
                    pO = [E4(nc.psum_tensor(f"pO{g}{i}", [128, 512], F32))
                          for i in range(2)]

                    first_w = [r_qrope]
                    for h in range(2):
                        hg = 2 * g + h
                        for j in range(8):
                            Kj = K_SLOT[j]
                            nst = Kj // 512
                            nkt = Kj // 128
                            scp = (h * 8 + j) % 2
                            jpar = j % 2
                            r_scev = [None] * nst
                            for ks in range(nst):
                                def f_smm(e, h=h, j=j, ks=ks, g=g):
                                    return e.matmul(
                                        pS[ks % 4][:, :],
                                        Qf[:, h * 1024 + j * 128: h * 1024 + (j + 1) * 128],
                                        Kf[:, h * 4096 + ks * 512: h * 4096 + (ks + 1) * 512],
                                        start=True, stop=True)
                                rsm = P.op("tensor", f_smm,
                                           waits=first_w + [
                                               r_scev[ks - 4] if ks >= 4 else
                                               r_pS_free[ks % 4]],
                                           incs=[(vt, 1)])
                                first_w = []

                                def f_sev(e, scp=scp, ks=ks, g=g):
                                    return e.tensor_copy(
                                        scores[scp][:, ks * 512:(ks + 1) * 512],
                                        pS[ks % 4][:, :])
                                r_scev[ks] = P.op("vector", f_sev,
                                                  waits=[rsm, r_sc_free[scp],
                                                         r_own_ev if g == 0 else None],
                                                  incs=[(vv, 1)])
                                r_pS_free[ks % 4] = r_scev[ks]

                            def f_mask(e, scp=scp, j=j, Kj=Kj):
                                return e.tensor_add(
                                    scores[scp][:, Kj - 512:Kj],
                                    scores[scp][:, Kj - 512:Kj],
                                    maskt[:, j * 512:(j + 1) * 512])
                            rgmk = P.op("gpsimd", f_mask,
                                        waits=[r_scev[nst - 1]], incs=[(vg, 1)])
                            if debug and g == 0 and h == 0 and j == 0:
                                rgmk = P.op("gpsimd", ld(dbg["sc"], scores[scp][:, :]),
                                            waits=[rgmk], incs=[(gq, 16)])

                            def f_max(e, scp=scp, Kj=Kj):
                                return e.reduce_max(smax[:, 0:1],
                                                    scores[scp][:, 0:Kj],
                                                    axis=AX.X, negate=True)
                            rgm = P.op("vector", f_max,
                                       waits=[rgmk, r_exp_prev], incs=[(vv, 1)])

                            def f_exp(e, scp=scp, Kj=Kj):
                                return e.activation(probs[:, 0:Kj],
                                                    scores[scp][:, 0:Kj], AF.Exp,
                                                    bias=smax[:, 0:1])
                            rex = P.op("scalar", f_exp, waits=[rgm, r_probs_free],
                                       incs=[(vs, 1)])
                            r_sc_free[scp] = rex
                            r_exp_prev = rex

                            def f_rsum(e, Kj=Kj):
                                return e.reduce_sum(smax[:, 1:2], probs[:, 0:Kj],
                                                    axis=AX.X)
                            rsum = P.op("vector", f_rsum, waits=[rex], incs=[(vv, 1)])

                            def f_rden(e):
                                return e.reciprocal(smax[:, 2:3], smax[:, 1:2])
                            rnoa = P.op("vector", f_rden, waits=[rsum], incs=[(vv, 1)])

                            def f_norm(e, Kj=Kj):
                                return e.tensor_scalar(probs[:, 0:Kj],
                                                       probs[:, 0:Kj],
                                                       smax[:, 2:3], None, ALU.mult)
                            rno = P.op("vector", f_norm, waits=[rnoa], incs=[(vv, 1)])
                            if debug and g == 0 and h == 0 and j == 0:
                                rno = P.op("gpsimd", ld(dbg["pr"], probs[:, :]),
                                           waits=[rno], incs=[(gq, 16)])

                            nb = (nkt + 7) // 8
                            r_ptev = [None] * nb
                            r_avb = None
                            for b in range(nb):
                                kts = list(range(b * 8, min((b + 1) * 8, nkt)))

                                def f_tr(e, kts=kts, b=b, g=g):
                                    ins = None
                                    for kt in kts:
                                        ins = e.transpose(
                                            pTt[b % 2][:, (kt % 8) * 128:(kt % 8 + 1) * 128],
                                            probs[:, kt * 128:(kt + 1) * 128],
                                            idn_b[:, :])
                                    return ins
                                rtr = P.op("tensor", f_tr,
                                           waits=[rno,
                                                  r_ptev[b - 2] if b >= 2 else None],
                                           incs=[(vt, 1)])

                                def f_ptev(e, kts=kts, b=b, g=g):
                                    n = len(kts) * 128
                                    return e.copy(
                                        ptb[:, (b % 2) * 1024:(b % 2) * 1024 + n],
                                        pTt[b % 2][:, 0:n])
                                r_ptev[b] = P.op("scalar", f_ptev, waits=[rtr],
                                                 incs=[(vs, 1)])
                                r_ptev_last = r_ptev[b]

                                def f_av(e, kts=kts, b=b, h=h, jpar=jpar, nkt=nkt, g=g):
                                    ins = None
                                    for kt in kts:
                                        ins = e.matmul(
                                            pO[jpar][:, 0:128],
                                            Vt[:, kt * 256 + h * 128: kt * 256 + (h + 1) * 128],
                                            ptb[:, (b % 2) * 1024 + (kt % 8) * 128:
                                                (b % 2) * 1024 + (kt % 8 + 1) * 128],
                                            start=(kt == 0), stop=(kt == nkt - 1))
                                    return ins
                                r_avb = P.op("tensor", f_av,
                                             waits=[r_ptev[b],
                                                    r_pO_free[jpar] if b == 0 else None],
                                             incs=[(vt, 1)])
                            r_probs_free = r_avb

                            def f_oev(e, hg=hg, j=j, jpar=jpar, g=g):
                                return e.tensor_copy(
                                    heads[:, hg * 1024 + j * 128: hg * 1024 + (j + 1) * 128],
                                    pO[jpar][:, 0:128])
                            roev = P.op("vector", f_oev, waits=[r_avb], incs=[(vv, 1)])
                            r_pO_free[jpar] = roev
                            r_oev_last = roev
                    r_prev_attn_done = r_oev_last
                r_dbg_hold_prev = r_dbg_hold
                P.flush(blk)

            if debug:
                r_oev_last = P.op("gpsimd", ld(dbg["hd"], heads[:, :]),
                                  waits=[r_oev_last], incs=[(gq, 16)])
            P.flush(blk)

        with ExitStack() as accs:
            EA = accs.enter_context
            acc = EA(nc.sbuf_tensor("acc", [128, 16 * 1024], F32))
            dsti2 = EA(nc.sbuf_tensor("dsti2", [128, 8], mybir.dt.int32))
            with ExitStack() as p3s:
                E5 = p3s.enter_context
                wob = [E5(nc.sbuf_tensor(f"wob{i}", [128, 2048], BF16)) for i in range(2)]
                pAO = [E5(nc.psum_tensor(f"pAO{i}", [128, 512], F32)) for i in range(2)]
                r_ao = {}
                r_aoev = {}
                r_wo = [None, None]
                for m in range(16):
                    extra_w = [r_oev_last, r_ptev_last] if m == 0 else []
                    r_wo[m % 2] = P.op("gpsimd", ld(wob[m % 2][:, :], d_wo[m]),
                                       waits=[r_ao.get((m - 2, 1))] + extra_w,
                                       incs=[(gq, 16)])
                    for s2 in range(2):
                        def f_ao(e, m=m, s2=s2):
                            ins = None
                            for c in range(16):
                                ins = e.matmul(
                                    pAO[s2][:, :],
                                    wob[m % 2][:, c * 128:(c + 1) * 128],
                                    heads[:, c * 1024 + s2 * 512: c * 1024 + (s2 + 1) * 512],
                                    start=(c == 0), stop=(c == 15))
                            return ins
                        extra = ([r_oev_last, r_ptev_last] if m == 0 and s2 == 0
                                 else [])
                        r_ao[(m, s2)] = P.op("tensor", f_ao,
                                             waits=[r_wo[m % 2],
                                                    r_aoev.get((m - 1, s2))] + extra,
                                             incs=[(vt, 1)])

                        def f_aoev(e, m=m, s2=s2):
                            return e.tensor_copy(
                                acc[:, m * 1024 + s2 * 512: m * 1024 + (s2 + 1) * 512],
                                pAO[s2][:, :])
                        r_aoev[(m, s2)] = P.op("vector", f_aoev,
                                               waits=[r_ao[(m, s2)]], incs=[(vv, 1)])
                r_oproj_t = r_ao[(15, 1)]
                r_oproj_v = r_aoev[(15, 1)]
                if debug:
                    r_oproj_v = P.op("gpsimd", ld(dbg["ac1"], acc[:, :]),
                                     waits=[r_oproj_v], incs=[(gq, 16)])
                P.flush(blk)

            with ExitStack() as p4s:
                E6 = p4s.enter_context
                xnf = [E6(nc.sbuf_tensor(f"xnf{i}", [128, 1024], BF16)) for i in range(2)]
                gwb = [E6(nc.sbuf_tensor(f"gwb{i}", [128, 2048], BF16)) for i in range(2)]
                uwb = [E6(nc.sbuf_tensor(f"uwb{i}", [128, 2048], BF16)) for i in range(2)]
                dwb = [E6(nc.sbuf_tensor(f"dwb{i}", [128, 2048], BF16)) for i in range(2)]
                tsil = [E6(nc.sbuf_tensor(f"tsil{i}", [128, 512], F32)) for i in range(2)]
                rst = E6(nc.sbuf_tensor("rst", [128, 24], F32))
                dtm = E6(nc.sbuf_tensor("dtm", [128, 24], F32))
                em_b = E6(nc.sbuf_tensor("em_b", [128, 64], BF16))
                lacc = E6(nc.sbuf_tensor("lacc", [128, 64], F32))
                trilb = E6(nc.sbuf_tensor("trilb", [128, 128], BF16))
                evalsb = E6(nc.sbuf_tensor("evalsb", [128, 8], BF16))
                dsti = E6(nc.sbuf_tensor("dsti", [128, 8], mybir.dt.int32))
                routw = E6(nc.sbuf_tensor("routw_s", [128, 16 * 8], BF16))

                r_trl = P.op("gpsimd", ld(trilb[:, :], d_tril), incs=[(gq, 16)])
                r_evl = P.op("gpsimd", ld(evalsb[:, :], d_evals), incs=[(gq, 16)])
                r_rwl = None
                for c in range(16):
                    r_rwl = P.op("gpsimd", ld(routw[:, c * 8:(c + 1) * 8],
                                              d_rw[c * 128:(c + 1) * 128, :]),
                                 incs=[(gq, 16)])

                pQY = [E6(nc.psum_tensor(f"pQY{i}", [128, 512], F32)) for i in range(4)]
                pXT = [E6(nc.psum_tensor(f"pXT{i}", [128, 1024], BF16)) for i in range(2)]

                r_idx2 = None
                with ExitStack() as rts:
                    E7 = rts.enter_context
                    pL = E7(nc.psum_tensor("pL", [128, 512], F32))
                    r_rx = [None, None]
                    r_rmm = None
                    r_racc = None
                    r_rmm_hist = [None, None]
                    for c in range(16):
                        extra_x = [r_oproj_t] if c == 0 else []
                        r_rx[c % 2] = P.op("sync", ld(xnf[c % 2][:, :], own_scr[c]),
                                           waits=[r_own_wr, r_rmm_hist[c % 2]]
                                           + extra_x,
                                           incs=[(sq, 16)])

                        def f_rmm(e, c=c):
                            ins = None
                            for j in range(8):
                                ins = e.matmul(pL[:, j * 8:(j + 1) * 8],
                                               xnf[c % 2][:, j * 128:(j + 1) * 128],
                                               routw[:, c * 8:(c + 1) * 8],
                                               start=True, stop=True)
                            return ins
                        extra = ([r_oproj_t, r_oproj_v, r_rwl] if c == 0 else [])
                        r_rmm = P.op("tensor", f_rmm,
                                     waits=[r_rx[c % 2], r_racc] + extra,
                                     incs=[(vt, 1)])
                        r_rmm_hist[c % 2] = r_rmm

                        def f_racc(e, c=c):
                            if c == 0:
                                return e.tensor_copy(lacc[:, :], pL[:, 0:64])
                            return e.tensor_add(lacc[:, :], pL[:, 0:64], lacc[:, :])
                        r_racc = P.op("vector", f_racc, waits=[r_rmm, r_racc],
                                      incs=[(vv, 1)])

                    def f_mx(e):
                        ins = None
                        for j in range(8):
                            ins = e.reduce_max(rst[:, j:j + 1],
                                               lacc[:, j * 8:(j + 1) * 8], axis=AX.X)
                        return ins
                    r_mx = P.op("vector", f_mx, waits=[r_racc], incs=[(vv, 1)])

                    def f_sig(e):
                        return e.activation(rst[:, 8:16], rst[:, 0:8], AF.Sigmoid)
                    r_sg_ = P.op("scalar", f_sig, waits=[r_mx], incs=[(vs, 1)])

                    r_cast_last = None
                    for j in range(8):
                        def f_em(e, j=j):
                            return e.tensor_scalar(em_b[:, j * 8:(j + 1) * 8],
                                                   lacc[:, j * 8:(j + 1) * 8],
                                                   rst[:, j:j + 1], None,
                                                   ALU.is_equal)
                        r_em = P.op("vector", f_em, waits=[r_sg_], incs=[(vv, 1)])

                        def f_rk(e, j=j):
                            return e.matmul(pL[:, 64 + j * 8: 64 + (j + 1) * 8],
                                            trilb[:, :],
                                            em_b[:, j * 8:(j + 1) * 8],
                                            start=True, stop=True)
                        r_rk = P.op("tensor", f_rk, waits=[r_em, r_trl, r_racc],
                                    incs=[(vt, 1)])

                        def f_d1(e, j=j):
                            e.tensor_mul(dtm[:, 0:8], em_b[:, j * 8:(j + 1) * 8],
                                         pL[:, 64 + j * 8: 64 + (j + 1) * 8])
                            return e.tensor_mul(dtm[:, 8:16],
                                                em_b[:, j * 8:(j + 1) * 8],
                                                evalsb[:, :])
                        r_d1 = P.op("vector", f_d1, waits=[r_rk, r_evl, r_cast_last],
                                    incs=[(vv, 1)])

                        def f_d2(e):
                            return e.tensor_add(dtm[:, 16:24], dtm[:, 0:8],
                                                dtm[:, 8:16])
                        r_d2 = P.op("vector", f_d2, waits=[r_d1], incs=[(vv, 1)])

                        def f_d3(e):
                            e.reduce_sum(rst[:, 16:17], dtm[:, 16:24], axis=AX.X)
                            return e.reduce_sum(rst[:, 18:19], dtm[:, 0:8], axis=AX.X)
                        r_d3 = P.op("vector", f_d3, waits=[r_d2], incs=[(vv, 1)])

                        def f_d4(e, j=j):
                            e.tensor_scalar(rst[:, 17:18], rst[:, 16:17], 1.0,
                                            float(j * CAPT), ALU.mult, ALU.add)
                            return e.tensor_scalar(rst[:, 19:20], rst[:, 18:19],
                                                   float(CAPT) - 0.5, None,
                                                   ALU.is_gt)
                        r_d4 = P.op("vector", f_d4, waits=[r_d3], incs=[(vv, 1)])

                        def f_d5(e):
                            return e.scalar_tensor_tensor(rst[:, 20:21],
                                                          rst[:, 19:20], 1.0e6,
                                                          rst[:, 17:18],
                                                          ALU.mult, ALU.add)
                        r_d5 = P.op("vector", f_d5, waits=[r_d4], incs=[(vv, 1)])

                        def f_d6(e, j=j):
                            return e.tensor_copy(dsti[:, j:j + 1], rst[:, 20:21])
                        r_cast_last = P.op("vector", f_d6, waits=[r_d5],
                                           incs=[(vv, 1)])
                    P.flush(blk)

                r_idx_o = P.op("sync", ld(idx_scr, dsti[:, :]),
                               waits=[r_cast_last], incs=[(sq, 16)])
                r_idx2 = P.op("sync", ld(dsti2[:, :], idx_scr),
                              waits=[r_idx_o], incs=[(sq, 16)])

                r_z_wr_last = None
                with ExitStack() as exsc:
                    E9 = exsc.enter_context
                    stile = [E9(nc.sbuf_tensor(f"stile{i}", [128, 2048], BF16))
                             for i in range(2)]
                    xec = E9(nc.sbuf_tensor("xec", [128, 16 * CAPE], BF16))
                    ztile = [E9(nc.sbuf_tensor(f"ztile{i}", [128, 2048], BF16))
                             for i in range(2)]
                    zfmb = E9(nc.sbuf_tensor("zfmb", [128, 16 * CAPE], BF16))

                    r_scat = None
                    r_sts = [None, None]
                    for j in range(8):
                        rsl = P.op("sync", ld(stile[j % 2][:, :],
                                              own_tm[j * 128:(j + 1) * 128, :]),
                                   waits=[r_own_wr, r_sts[j % 2]], incs=[(sq, 16)])

                        def f_sc(e, j=j):
                            return e.tensor_scalar(stile[j % 2][:, :],
                                                   stile[j % 2][:, :],
                                                   rst[:, 8 + j:9 + j], None,
                                                   ALU.mult)
                        rsc = P.op("vector", f_sc, waits=[rsl, r_cast_last],
                                   incs=[(vv, 1)])

                        def f_scat(e, j=j):
                            return e.indirect_dma_start(
                                out=cpq_scr,
                                out_offset=IndirectOffsetOnAxis(dsti2[:, j:j + 1],
                                                                axis=0),
                                in_=stile[j % 2][:, :], in_offset=None,
                                bounds_check=8 * CAPE, oob_is_err=False)
                        r_scat = P.op("gpsimd", f_scat, waits=[rsc, r_idx2],
                                      incs=[(gq, 16)])
                        r_sts[j % 2] = r_scat

                    r_xec_ev = None
                    r_y = {}
                    r_h = {}
                    r_zev_prev = None
                    r_sil_prev = None
                    r_ylast = None
                    r_zmm_hist = [None, None]
                    for eidx in range(E):
                        r_tr2 = [None, None]
                        for tt in range(3):
                            rct = P.op("sync", ld(
                                stile[tt % 2][:, :],
                                cpq_scr[eidx * CAPE + tt * 128:
                                        eidx * CAPE + (tt + 1) * 128, :]),
                                waits=[r_scat, r_tr2[tt % 2],
                                       r_sts[tt % 2] if eidx == 0 else None],
                                incs=[(sq, 16)])
                            for half in range(2):
                                def f_ct(e, tt=tt, half=half):
                                    ins = None
                                    for c8 in range(8):
                                        c = half * 8 + c8
                                        ins = e.transpose(
                                            pXT[half][:, c8 * 128:(c8 + 1) * 128],
                                            stile[tt % 2][:, c * 128:(c + 1) * 128],
                                            idn_b[:, :])
                                    return ins
                                rtr = P.op("tensor", f_ct,
                                           waits=[rct, r_idn, r_xec_ev,
                                                  r_zev_prev if tt == 0 and half == 0 else None],
                                           incs=[(vt, 1)])
                                r_tr2[tt % 2] = rtr

                                def f_cte(e, tt=tt, half=half):
                                    dst = xec.ap().rearrange(
                                        "p (c n) -> p c n", c=16)[
                                        :, half * 8:(half + 1) * 8,
                                        tt * 128:(tt + 1) * 128]
                                    return e.tensor_copy(
                                        dst, pXT[half].ap().rearrange(
                                            "p (c n) -> p c n", c=8))
                                r_xec_ev = P.op("vector", f_cte, waits=[rtr],
                                                incs=[(vv, 1)])

                        rw1 = [None, None]
                        rw2 = [None, None]
                        for m in range(16):
                            wv_ = [r_y.get(m - 2) if m >= 2 else r_ylast]
                            rw1[m % 2] = P.op("gpsimd",
                                              ld(gwb[m % 2][:, :], d_g[eidx][m]),
                                              waits=wv_, incs=[(gq, 16)])
                            rw2[m % 2] = P.op("gpsimd",
                                              ld(uwb[m % 2][:, :], d_u[eidx][m]),
                                              waits=wv_, incs=[(gq, 16)])

                            def f_y(e, m=m):
                                ins = None
                                for c in range(16):
                                    e.matmul(pQY[m % 2][:, 0:CAPE],
                                             gwb[m % 2][:, c * 128:(c + 1) * 128],
                                             xec[:, c * CAPE:(c + 1) * CAPE],
                                             start=(c == 0), stop=(c == 15))
                                for c in range(16):
                                    ins = e.matmul(pQY[2 + m % 2][:, 0:CAPE],
                                                   uwb[m % 2][:, c * 128:(c + 1) * 128],
                                                   xec[:, c * CAPE:(c + 1) * CAPE],
                                                   start=(c == 0), stop=(c == 15))
                                return ins
                            r_y[m] = P.op(
                                "tensor", f_y,
                                waits=[rw1[m % 2], rw2[m % 2], r_xec_ev,
                                       r_h.get(m - 2), r_zev_prev],
                                incs=[(vt, 1)])

                            def f_sil(e, m=m):
                                return e.activation(tsil[m % 2][:, 0:CAPE],
                                                    pQY[m % 2][:, 0:CAPE], AF.Silu)
                            rsil = P.op("scalar", f_sil, waits=[r_y[m]],
                                        incs=[(vs, 1)])
                            r_sil_prev = rsil

                            def f_h(e, m=m):
                                return e.tensor_mul(
                                    heads[:, m * CAPE:(m + 1) * CAPE],
                                    tsil[m % 2][:, 0:CAPE],
                                    pQY[2 + m % 2][:, 0:CAPE])
                            r_h[m] = P.op("vector", f_h,
                                          waits=[rsil, r_oproj_t], incs=[(vv, 1)])

                        rwd = [None, None]
                        r_zfm = {}
                        r_zfm_ev = {}
                        for n in range(16):
                            rwd[n % 2] = P.op("gpsimd", ld(
                                dwb[n % 2][:, :], d_dn[eidx][n]),
                                waits=[r_zfm.get(n - 2), r_zev_prev],
                                incs=[(gq, 16)])

                            def f_zmm(e, n=n):
                                ins = None
                                for m in range(16):
                                    ins = e.matmul(
                                        pQY[n % 2][:, 0:CAPE],
                                        dwb[n % 2][:, m * 128:(m + 1) * 128],
                                        heads[:, m * CAPE:(m + 1) * CAPE],
                                        start=(m == 0), stop=(m == 15))
                                return ins
                            wz = [rwd[n % 2], r_zfm_ev.get(n - 2)]
                            if n == 0:
                                wz += [r_h.get(15), r_sil_prev]
                            r_zfm[n] = P.op("tensor", f_zmm, waits=wz,
                                            incs=[(vt, 1)])

                            def f_zfev(e, n=n):
                                return e.tensor_copy(
                                    zfmb[:, n * CAPE:(n + 1) * CAPE],
                                    pQY[n % 2][:, 0:CAPE])
                            r_zfm_ev[n] = P.op("vector", f_zfev,
                                               waits=[r_zfm[n]], incs=[(vv, 1)])

                        r_ztev = None
                        r_zwr = [None, None]
                        for tt in range(3):
                            for half in range(2):
                                def f_zt(e, tt=tt, half=half):
                                    ins = None
                                    for c8 in range(8):
                                        n = half * 8 + c8
                                        ins = e.transpose(
                                            pXT[half][:, c8 * 128:(c8 + 1) * 128],
                                            zfmb[:, n * CAPE + tt * 128:
                                                 n * CAPE + (tt + 1) * 128],
                                            idn_b[:, :])
                                    return ins
                                rzt2 = P.op("tensor", f_zt,
                                            waits=[r_zfm_ev.get(15), r_idn,
                                                   r_ztev], incs=[(vt, 1)])

                                def f_ztev(e, tt=tt, half=half):
                                    return e.tensor_copy(
                                        ztile[tt % 2][:, half * 1024:(half + 1) * 1024],
                                        pXT[half][:, :])
                                r_ztev = P.op("vector", f_ztev,
                                              waits=[rzt2, r_zwr[tt % 2]],
                                              incs=[(vv, 1)])

                            def f_zwr(e, eidx=eidx, tt=tt):
                                return e.dma_start(
                                    out=z_scr[eidx * CAPE + tt * 128:
                                              eidx * CAPE + (tt + 1) * 128, :],
                                    in_=ztile[tt % 2][:, :])
                            r_zwr[tt % 2] = P.op("sync", f_zwr, waits=[r_ztev],
                                                 incs=[(sq, 16)])
                        r_zev = r_ztev
                        r_zev_prev = r_zev
                        r_ylast = r_y[15]
                        r_z_wr_last = max((r for r in r_zwr if r is not None),
                                          key=lambda r: r.tok()[1])
                        r_zmm_hist = [r_zfm.get(14), r_zfm.get(15)]
                        P.flush(blk)

                with ExitStack() as shsc:
                    E10 = shsc.enter_context
                    xe = E10(nc.sbuf_tensor("xe_sh", [128, 16 * 1024], BF16))
                    r_xe = None
                    for c in range(16):
                        r_xe = P.op("sync",
                                    ld(xe[:, c * 1024:(c + 1) * 1024], own_scr[c]),
                                    waits=[r_own_wr,
                                           r_zev_prev if c == 0 else None],
                                    incs=[(sq, 16)])
                    r_yd = {}
                    r_hd = {}
                    r_zd = {}
                    rw1 = [None, None]
                    rw2 = [None, None]
                    for m in range(16):
                        wv_ = [r_yd.get((m - 2, 1)) if m >= 2 else r_ylast]
                        rw1[m % 2] = P.op("gpsimd", ld(gwb[m % 2][:, :], d_sg[m]),
                                          waits=wv_, incs=[(gq, 16)])
                        rw2[m % 2] = P.op("gpsimd", ld(uwb[m % 2][:, :], d_su[m]),
                                          waits=wv_, incs=[(gq, 16)])
                        for s2 in range(2):
                            def f_y(e, m=m, s2=s2):
                                ins = None
                                for c in range(16):
                                    e.matmul(pQY[s2][:, :],
                                             gwb[m % 2][:, c * 128:(c + 1) * 128],
                                             xe[:, c * 1024 + s2 * 512: c * 1024 + (s2 + 1) * 512],
                                             start=(c == 0), stop=(c == 15))
                                for c in range(16):
                                    ins = e.matmul(pQY[2 + s2][:, :],
                                                   uwb[m % 2][:, c * 128:(c + 1) * 128],
                                                   xe[:, c * 1024 + s2 * 512: c * 1024 + (s2 + 1) * 512],
                                                   start=(c == 0), stop=(c == 15))
                                return ins
                            extra = ([r_zev_prev, r_z_wr_last] if m == 0 and s2 == 0
                                     else [])
                            r_yd[(m, s2)] = P.op(
                                "tensor", f_y,
                                waits=[rw1[m % 2], rw2[m % 2], r_xe,
                                       r_hd.get((m - 1, s2))] + extra,
                                incs=[(vt, 1)])

                            def f_sil(e, s2=s2):
                                return e.activation(tsil[s2][:, :], pQY[s2][:, :],
                                                    AF.Silu)
                            rsil = P.op("scalar", f_sil, waits=[r_yd[(m, s2)]],
                                        incs=[(vs, 1)])

                            def f_h(e, m=m, s2=s2):
                                return e.tensor_mul(
                                    heads[:, m * 1024 + s2 * 512: m * 1024 + (s2 + 1) * 512],
                                    tsil[s2][:, :], pQY[2 + s2][:, :])
                            r_hd[(m, s2)] = P.op("vector", f_h,
                                                 waits=[rsil, r_oproj_t],
                                                 incs=[(vv, 1)])
                    rwd = [None, None]
                    for n in range(16):
                        rwd[n % 2] = P.op("gpsimd", ld(dwb[n % 2][:, :], d_sd[n]),
                                          waits=[r_zd.get((n - 2, 1)) if n >= 2
                                                 else r_zev_prev],
                                          incs=[(gq, 16)])
                        for s2 in range(2):
                            def f_z(e, n=n, s2=s2):
                                ins = None
                                for m in range(16):
                                    ins = e.matmul(
                                        pQY[s2][:, :],
                                        dwb[n % 2][:, m * 128:(m + 1) * 128],
                                        heads[:, m * 1024 + s2 * 512: m * 1024 + (s2 + 1) * 512],
                                        start=(m == 0), stop=(m == 15))
                                return ins
                            rzt = P.op("tensor", f_z,
                                       waits=[rwd[n % 2], r_hd[(15, 0)], r_hd[(15, 1)],
                                              r_zd.get((n - 1, s2))],
                                       incs=[(vt, 1)])

                            def f_zev(e, n=n, s2=s2):
                                sl = acc[:, n * 1024 + s2 * 512: n * 1024 + (s2 + 1) * 512]
                                return e.tensor_add(sl, pQY[s2][:, :], sl)
                            r_zd[(n, s2)] = P.op("vector", f_zev,
                                                 waits=[rzt, r_oproj_v],
                                                 incs=[(vv, 1)])
                    r_zlast = r_zd[(15, 1)]
                if debug:
                    r_zlast = P.op("gpsimd", ld(dbg["ac2"], acc[:, :]),
                                   waits=[r_zlast], incs=[(gq, 16)])
                P.flush(blk)

            with ExitStack() as p5s:
                E8 = p5s.enter_context
                xob = [E8(nc.sbuf_tensor(f"xob{i}", [128, 2048], F32)) for i in range(2)]
                otb = [E8(nc.sbuf_tensor(f"otb{i}", [128, 2048], F32)) for i in range(2)]
                mg = [E8(nc.sbuf_tensor(f"mg{i}", [128, 2048], BF16)) for i in range(2)]
                pOut = [E8(nc.psum_tensor(f"pOut{i}", [128, 2048], F32))
                        for i in range(2)]
                r_add = [None, None]
                r_store = [None] * 8
                for j in range(8):
                    par = j % 2
                    rxo = P.op("sync", ld(xob[par][:, :],
                                          d_xown[j * 128:(j + 1) * 128, :]),
                               waits=[r_add[par],
                                      r_zlast if j == 0 else None],
                               incs=[(sq, 16)])

                    def f_mg(e, j=j, par=par):
                        return e.indirect_dma_start(
                            out=mg[par][:, :], out_offset=None,
                            in_=z_scr,
                            in_offset=IndirectOffsetOnAxis(dsti2[:, j:j + 1],
                                                           axis=0),
                            bounds_check=8 * CAPE, oob_is_err=False)
                    rmg = P.op("gpsimd", f_mg,
                               waits=[r_z_wr_last, r_idx2, r_add[par]],
                               incs=[(gq, 16)])

                    def f_tro(e, j=j, par=par):
                        ins = None
                        for c in range(16):
                            ins = e.transpose(
                                pOut[par][:, c * 128:(c + 1) * 128],
                                acc[:, c * 1024 + j * 128: c * 1024 + (j + 1) * 128],
                                idn_f[:, :])
                        return ins
                    rto = P.op("tensor", f_tro,
                               waits=[r_zlast, r_idnf, r_add[par]], incs=[(vt, 1)])

                    def f_add(e, par=par):
                        return e.tensor_add(otb[par][:, :], pOut[par][:, :],
                                            xob[par][:, :])
                    ra1 = P.op("vector", f_add,
                               waits=[rto, rxo,
                                      r_store[j - 2] if j >= 2 else None],
                               incs=[(vv, 1)])

                    def f_add2(e, par=par):
                        return e.tensor_add(otb[par][:, :], otb[par][:, :],
                                            mg[par][:, :])
                    r_add[par] = P.op("vector", f_add2, waits=[ra1, rmg],
                                      incs=[(vv, 1)])

                    r_store[j] = P.op("sync", ld(d_out[j], otb[par][:, :]),
                                      waits=[r_add[par]], incs=[(sq, 16)])

                P.op("sync", lambda e: e.nop(), waits=[r_store[-1]])
                P.flush(blk)

    return nc


_NC_CACHE = None


def _get_nc():
    global _NC_CACHE
    if _NC_CACHE is None:
        _NC_CACHE = build_nc()
    return _NC_CACHE



def _perm_lhsT(A):
    return np.ascontiguousarray(
        A.reshape(16, 128, 16, 128).transpose(2, 1, 0, 3).reshape(16, 128, 2048))


def kernel(L0_attn_X, L0_attn_Wq, L0_attn_Wk, L0_attn_Wv, L0_attn_Wo,
           L0_ffn_router_w, L0_ffn_gate_w, L0_ffn_up_w, L0_ffn_down_w,
           L0_ffn_shared_gate_w, L0_ffn_shared_up_w, L0_ffn_shared_down_w):
    f32 = lambda a: np.ascontiguousarray(np.asarray(a, dtype=np.float32))
    X = f32(L0_attn_X)

    wq_p = _perm_lhsT(f32(L0_attn_Wq).T)
    wk_p = _perm_lhsT(f32(L0_attn_Wk).T)
    wv_n = np.ascontiguousarray(f32(L0_attn_Wv).T)
    wo_p = _perm_lhsT(f32(L0_attn_Wo).T)
    rw_n = f32(L0_ffn_router_w)
    gate_p = np.stack([_perm_lhsT(f32(L0_ffn_gate_w)[e]) for e in range(E)])
    up_p = np.stack([_perm_lhsT(f32(L0_ffn_up_w)[e]) for e in range(E)])
    down_p = np.stack([_perm_lhsT(f32(L0_ffn_down_w)[e]) for e in range(E)])
    tril_t = np.ascontiguousarray(np.tril(np.ones((128, 128), np.float32), -1).T)
    evals_t = np.ascontiguousarray(
        np.tile((np.arange(8, dtype=np.float32) * CAPE)[None, :], (128, 1)))
    sg_p = _perm_lhsT(f32(L0_ffn_shared_gate_w))
    su_p = _perm_lhsT(f32(L0_ffn_shared_up_w))
    sd_p = _perm_lhsT(f32(L0_ffn_shared_down_w))

    theta = (10000.0 ** (-2.0 * np.arange(DH // 2, dtype=np.float32) / DH)
             ).astype(np.float32)
    pos = np.arange(T, dtype=np.float32)
    ang = theta[:, None] * pos[None, :]
    cos64 = np.cos(ang).astype(np.float32)
    sin64 = np.sin(ang).astype(np.float32)
    cosk = np.repeat(cos64, 2, axis=0)
    sink = np.empty_like(cosk)
    sink[0::2] = -sin64
    sink[1::2] = sin64
    scale = np.float32(DH ** -0.5)

    pswp = np.zeros((128, 128), np.float32)
    for i in range(64):
        pswp[2 * i + 1, 2 * i] = 1.0
        pswp[2 * i, 2 * i + 1] = 1.0

    sel8 = np.zeros((8, 1024), np.float32)
    for e in range(8):
        sel8[e, e * 128:(e + 1) * 128] = 1.0

    nc = _get_nc()
    in_maps = []
    for core in range(8):
        b, q = core // 4, core % 4
        tiles = own_tiles_for(q)
        xown = np.ascontiguousarray(
            np.concatenate([X[b, t * 128:(t + 1) * 128] for t in tiles], axis=0))
        qpos = np.concatenate([t * 128 + np.arange(128) for t in tiles])
        cosq = np.ascontiguousarray(cosk[:, qpos] * scale)
        sinq = np.ascontiguousarray(sink[:, qpos] * scale)
        maskt = np.zeros((8, 128, 512), np.float32)
        for j, t in enumerate(tiles):
            Kj = K_SLOT[j]
            kcols = Kj - 512 + np.arange(512)
            qrows = t * 128 + np.arange(128)
            maskt[j] = np.where(kcols[None, :] > qrows[:, None], -30000.0, 0.0)
        in_maps.append(dict(
            xseq=np.ascontiguousarray(X[b]), xown=xown,
            cosk=np.ascontiguousarray(cosk), sink=np.ascontiguousarray(sink),
            cosq=cosq, sinq=sinq, maskt=maskt, pswp=pswp, sel8=sel8,
            wq_p=wq_p, wk_p=wk_p, wv_n=wv_n, wo_p=wo_p, rw_n=rw_n,
            gate_p=gate_p, up_p=up_p, down_p=down_p,
            tril_t=tril_t, evals_t=evals_t,
            sg_p=sg_p, su_p=su_p, sd_p=sd_p))

    res = run_bass_kernel_spmd(nc, in_maps, core_ids=list(range(8)))

    out = np.empty((B, T, D), np.float32)
    for core in range(8):
        b, q = core // 4, core % 4
        tiles = own_tiles_for(q)
        oo = res.results[core]["out_o"]
        for j, t in enumerate(tiles):
            out[b, t * 128:(t + 1) * 128, :] = oo[j]
    return out




# version 12
# speedup vs baseline: 1.1520x; 1.1520x over previous
import numpy as np
import concourse.bass as bass
from concourse.bass import IndirectOffsetOnAxis
import concourse.mybir as mybir
from concourse.bass_utils import run_bass_kernel_spmd
from contextlib import ExitStack

F32 = mybir.dt.float32
BF16 = mybir.dt.bfloat16
AF = mybir.ActivationFunctionType
ALU = mybir.AluOpType
AX = mybir.AxisListType

B, T, D, H, DH, E = 2, 4096, 2048, 16, 128, 8
NCH = 16
NT = 32
NSTR = 8
NG = 8
K_SLOT = [(32 - 4 * j) * 128 for j in range(8)]
EPS = 1e-5
CAPT = 48
CAPE = 8 * CAPT


def own_tiles_for(q):
    return [31 - (4 * j + (q if j % 2 == 0 else 3 - q)) for j in range(8)]



class Sem:
    def __init__(self, handle, name):
        self.h, self.name, self.n = handle, name, 0


class Rec:
    __slots__ = ("eng", "fn", "waits", "incs", "after")

    def __init__(self, eng, fn, waits, incs):
        self.eng, self.fn, self.waits, self.incs = eng, fn, list(waits), list(incs)
        self.after = None

    def tok(self):
        assert self.after is not None
        return self.after


class Prog:
    def __init__(self):
        self.ops = []
        self.last_wait = {}

    def op(self, eng, fn, waits=(), incs=()):
        wl = []
        for w in waits:
            if w is None:
                continue
            if isinstance(w, Rec):
                w = w.tok()
            sem, cnt = w
            assert cnt <= sem.n, f"wait on future count {cnt} > {sem.n} ({sem.name})"
            wl.append((sem, cnt))
        r = Rec(eng, fn, wl, incs)
        for sem, k in incs:
            sem.n += k
            if r.after is None:
                r.after = (sem, sem.n)
        self.ops.append(r)
        return r

    def flush(self, block):
        ops, self.ops = self.ops, []
        by_eng = {}
        for r in ops:
            by_eng.setdefault(r.eng, []).append(r)
        for engname, recs in by_eng.items():
            def run(eng, recs=recs, engname=engname):
                for r in recs:
                    for sem, cnt in r.waits:
                        key = (engname, sem.name)
                        if self.last_wait.get(key, -1) < cnt:
                            eng.wait_ge(sem.h, cnt)
                            self.last_wait[key] = cnt
                    ins = r.fn(eng)
                    for sem, k in r.incs:
                        assert ins is not None
                        ins.then_inc(sem.h, k)
            getattr(block, engname)(run)



def build_nc(debug=False):
    nc = bass.Bass("TRN2", target_bir_lowering=False)
    dbg = {}
    if debug:
        dbg["xn"] = nc.dram_tensor("dbg_xn", [16, 128, 1024], F32, kind="ExternalOutput").ap()
        dbg["fm0"] = nc.dram_tensor("dbg_fm0", [16, 128, 512], F32, kind="ExternalOutput").ap()
        dbg["kf"] = nc.dram_tensor("dbg_kf", [128, 8192], F32, kind="ExternalOutput").ap()
        dbg["vt"] = nc.dram_tensor("dbg_vt", [128, 8192], F32, kind="ExternalOutput").ap()
        dbg["qf"] = nc.dram_tensor("dbg_qf", [128, 2048], F32, kind="ExternalOutput").ap()
        dbg["sc"] = nc.dram_tensor("dbg_sc", [128, 4096], F32, kind="ExternalOutput").ap()
        dbg["pr"] = nc.dram_tensor("dbg_pr", [128, 4096], F32, kind="ExternalOutput").ap()
        dbg["hd"] = nc.dram_tensor("dbg_hd", [128, 16384], F32, kind="ExternalOutput").ap()
        dbg["ac1"] = nc.dram_tensor("dbg_ac1", [128, 16384], F32, kind="ExternalOutput").ap()
        dbg["ac2"] = nc.dram_tensor("dbg_ac2", [128, 16384], F32, kind="ExternalOutput").ap()
        dbg["gmt"] = nc.dram_tensor("dbg_gmt", [8, 1024], F32, kind="ExternalOutput").ap()
        dbg["xe"] = nc.dram_tensor("dbg_xe", [128, 16384], F32, kind="ExternalOutput").ap()

    d_xseq = nc.dram_tensor("xseq", [T, D], F32, kind="ExternalInput").ap()
    d_xown = nc.dram_tensor("xown", [8 * 128, D], F32, kind="ExternalInput").ap()
    d_cosk = nc.dram_tensor("cosk", [128, T], F32, kind="ExternalInput").ap()
    d_sink = nc.dram_tensor("sink", [128, T], F32, kind="ExternalInput").ap()
    d_cosq = nc.dram_tensor("cosq", [128, 1024], F32, kind="ExternalInput").ap()
    d_sinq = nc.dram_tensor("sinq", [128, 1024], F32, kind="ExternalInput").ap()
    d_mask = nc.dram_tensor("maskt", [8, 128, 512], F32, kind="ExternalInput").ap()
    d_pswp = nc.dram_tensor("pswp", [128, 128], F32, kind="ExternalInput").ap()
    d_sel = nc.dram_tensor("sel8", [8, 1024], F32, kind="ExternalInput").ap()
    d_wq = nc.dram_tensor("wq_p", [16, 128, 2048], F32, kind="ExternalInput").ap()
    d_wk = nc.dram_tensor("wk_p", [16, 128, 2048], F32, kind="ExternalInput").ap()
    d_wv = nc.dram_tensor("wv_n", [2048, 2048], F32, kind="ExternalInput").ap()
    d_wo = nc.dram_tensor("wo_p", [16, 128, 2048], F32, kind="ExternalInput").ap()
    d_rw = nc.dram_tensor("rw_n", [2048, 8], F32, kind="ExternalInput").ap()
    d_g = nc.dram_tensor("gate_p", [8, 16, 128, 2048], F32, kind="ExternalInput").ap()
    d_u = nc.dram_tensor("up_p", [8, 16, 128, 2048], F32, kind="ExternalInput").ap()
    d_dn = nc.dram_tensor("down_p", [8, 16, 128, 2048], F32, kind="ExternalInput").ap()
    d_tril = nc.dram_tensor("tril_t", [128, 128], F32, kind="ExternalInput").ap()
    d_evals = nc.dram_tensor("evals_t", [128, 8], F32, kind="ExternalInput").ap()
    d_sg = nc.dram_tensor("sg_p", [16, 128, 2048], F32, kind="ExternalInput").ap()
    d_su = nc.dram_tensor("su_p", [16, 128, 2048], F32, kind="ExternalInput").ap()
    d_sd = nc.dram_tensor("sd_p", [16, 128, 2048], F32, kind="ExternalInput").ap()
    d_out = nc.dram_tensor("out_o", [8, 128, 2048], F32, kind="ExternalOutput").ap()

    fm_scr = nc.dram_tensor("fm_scr", [NSTR, NCH, 128, 512], BF16, kind="Internal").ap()
    own_scr = nc.dram_tensor("own_scr", [NCH, 128, 1024], BF16, kind="Internal").ap()
    own_tm = nc.dram_tensor("own_tm", [1024, 2048], BF16, kind="Internal").ap()
    cpq_scr = nc.dram_tensor("cpq_scr", [8 * CAPE, 2048], BF16, kind="Internal").ap()
    z_scr = nc.dram_tensor("z_scr", [8 * CAPE, 2048], BF16, kind="Internal").ap()
    idx_scr = nc.dram_tensor("idx_scr", [128, 8], mybir.dt.int32, kind="Internal").ap()

    P = Prog()

    with ExitStack() as top:
        EN = top.enter_context
        blk = EN(nc.Block())

        sq = Sem(EN(nc.semaphore("sq")), "sq")
        gq = Sem(EN(nc.semaphore("gq")), "gq")
        vt = Sem(EN(nc.semaphore("vt")), "vt")
        vv = Sem(EN(nc.semaphore("vv")), "vv")
        vs = Sem(EN(nc.semaphore("vs")), "vs")
        vg = Sem(EN(nc.semaphore("vg")), "vg")

        idn_b = EN(nc.sbuf_tensor("idn_b", [128, 128], BF16))
        idn_f = EN(nc.sbuf_tensor("idn_f", [128, 128], F32))
        heads = EN(nc.sbuf_tensor("heads", [128, 16 * 1024], BF16))

        def ld(dst_ap, src_ap):
            return lambda e: e.dma_start(out=dst_ap, in_=src_ap)

        def mk_idn(idn):
            def f(e):
                e.memset(idn[:, :], 0.0)
                return e.affine_select(out=idn[:, :], in_=idn[:, :],
                                       compare_op=ALU.not_equal, fill=1.0,
                                       base=0, pattern=[[-1, 128]],
                                       channel_multiplier=1)
            return f
        r_idn = P.op("gpsimd", mk_idn(idn_b), incs=[(vg, 1)])
        r_idnf = P.op("gpsimd", mk_idn(idn_f), incs=[(vg, 1)])

        def emit_ln(pfx, src, tiles, strip_cols, dst_view, scope, first_waits,
                    tm_dst=None):
            EL = scope.enter_context
            xb = [EL(nc.sbuf_tensor(f"{pfx}xb{i}", [128, D], F32)) for i in range(2)]
            xnb = [EL(nc.sbuf_tensor(f"{pfx}xnb{i}", [128, D], BF16)) for i in range(2)]
            junk = EL(nc.sbuf_tensor(f"{pfx}junk", [128, D], F32))
            stat = [EL(nc.sbuf_tensor(f"{pfx}st{i}", [128, 16], F32)) for i in range(2)]
            strip = EL(nc.sbuf_tensor(f"{pfx}strip", [128, 16 * strip_cols], BF16))
            pT = [EL(nc.psum_tensor(f"{pfx}pT{i}", [128, 1024], BF16)) for i in range(4)]
            per_strip = strip_cols // 128
            r_junk_free = [None]
            r_read = [None, None]
            r_tm = [None, None]
            r_tr = [None, None, None, None]
            r_ev = [None, None]
            r_wr = None
            recs = {}
            fw = list(first_waits)
            for it, t in enumerate(tiles):
                par = it % 2
                rx = P.op("sync", ld(xb[par][:, :], src[t * 128:(t + 1) * 128, :]),
                          waits=[r_read[par]], incs=[(sq, 16)])
                s1 = stat[par][:, 0:1]; s2 = stat[par][:, 1:2]
                mean = stat[par][:, 2:3]; em2 = stat[par][:, 3:4]
                msq = stat[par][:, 4:5]; var = stat[par][:, 5:6]
                sd_ = stat[par][:, 6:7]; rstd = stat[par][:, 7:8]
                nmr = stat[par][:, 8:9]

                def f_sum(e, par=par, s1=s1):
                    return e.reduce_sum(s1, xb[par][:, :], axis=AX.X)
                P.op("vector", f_sum, waits=[rx], incs=[(vv, 1)])

                def f_sq(e, par=par):
                    return e.activation(junk[:, :], xb[par][:, :], AF.Square)
                rs1 = P.op("scalar", f_sq, waits=[rx, r_junk_free[0]],
                           incs=[(vs, 1)])

                def f_sqsum(e, s2=s2):
                    return e.reduce_sum(s2, junk[:, :], axis=AX.X)
                rs1 = P.op("vector", f_sqsum, waits=[rs1], incs=[(vv, 1)])
                r_junk_free[0] = rs1

                def f_st1(e, s1=s1, s2=s2, mean=mean, em2=em2):
                    e.tensor_scalar_mul(mean, s1, 1.0 / D)
                    return e.tensor_scalar(em2, s2, 1.0 / D, EPS,
                                           ALU.mult, ALU.add)
                rv2a = P.op("vector", f_st1, waits=[rs1], incs=[(vv, 1)])

                def f_st2(e, mean=mean, msq=msq):
                    return e.tensor_mul(msq, mean, mean)
                rv2b = P.op("vector", f_st2, waits=[rv2a], incs=[(vv, 1)])

                def f_st3(e, msq=msq, em2=em2, var=var):
                    return e.scalar_tensor_tensor(var, msq, -1.0, em2,
                                                  ALU.mult, ALU.add)
                rv2 = P.op("vector", f_st3, waits=[rv2b], incs=[(vv, 1)])

                def f_sd(e, var=var, sd_=sd_):
                    return e.activation(sd_, var, AF.Sqrt)
                rs2 = P.op("scalar", f_sd, waits=[rv2], incs=[(vs, 1)])

                def f_recip(e, sd_=sd_, rstd=rstd):
                    return e.reciprocal(rstd, sd_)
                rv3a = P.op("vector", f_recip, waits=[rs2], incs=[(vv, 1)])

                def f_nmr(e, rstd=rstd, mean=mean, nmr=nmr):
                    return e.scalar_tensor_tensor(nmr, mean, -1.0, rstd,
                                                  ALU.mult, ALU.mult)
                rv3 = P.op("vector", f_nmr, waits=[rv3a], incs=[(vv, 1)])

                def f_xn(e, par=par, rstd=rstd, nmr=nmr):
                    return e.tensor_scalar(xnb[par][:, :], xb[par][:, :],
                                           rstd, nmr, ALU.mult, ALU.add)
                rs3 = P.op("vector", f_xn,
                           waits=[rv3, r_tr[par * 2], r_tr[par * 2 + 1],
                                  r_tm[par]],
                           incs=[(vv, 1)])
                r_read[par] = rs3

                if tm_dst is not None:
                    def f_tm(e, it=it, par=par):
                        return e.dma_start(out=tm_dst[it * 128:(it + 1) * 128, :],
                                           in_=xnb[par][:, :])
                    r_tm[par] = P.op("sync", f_tm, waits=[rs3], incs=[(sq, 16)])

                for half in range(2):
                    hidx = par * 2 + half

                    def f_tr(e, par=par, half=half, hidx=hidx):
                        ins = None
                        for c8 in range(8):
                            c = half * 8 + c8
                            ins = e.transpose(pT[hidx][:, c8 * 128:(c8 + 1) * 128],
                                              xnb[par][:, c * 128:(c + 1) * 128],
                                              idn_b[:, :])
                        return ins
                    r_tr[hidx] = P.op("tensor", f_tr,
                                      waits=[rs3, r_idn, r_ev[par]] + fw,
                                      incs=[(vt, 1)])
                    fw = []

                pos = it % per_strip

                def f_ev(e, par=par, pos=pos):
                    ins = None
                    for half in range(2):
                        hidx = par * 2 + half
                        dst = strip.ap().rearrange("p (c n) -> p c n", c=16)[
                            :, half * 8:(half + 1) * 8, pos * 128:(pos + 1) * 128]
                        ins = e.tensor_copy(dst, pT[hidx].ap().rearrange(
                            "p (c n) -> p c n", c=8))
                    return ins
                rev = P.op("vector", f_ev,
                           waits=[r_tr[par * 2], r_tr[par * 2 + 1], r_wr],
                           incs=[(vv, 1)])
                r_ev[par] = rev

                if pos == per_strip - 1:
                    sidx = it // per_strip

                    def f_wr(e, sidx=sidx):
                        return e.dma_start(out=dst_view(sidx),
                                           in_=strip.ap().rearrange(
                                               "p (c n) -> p c n", c=16))
                    r_wr = P.op("sync", f_wr, waits=[r_ev[0], r_ev[1]],
                                incs=[(sq, 16)])
                    recs[sidx] = r_wr
            return recs, rev

        with ExitStack() as p2s:
            E2 = p2s.enter_context
            cosk = E2(nc.sbuf_tensor("cosk_s", [128, T], BF16))
            sink = E2(nc.sbuf_tensor("sink_s", [128, T], BF16))
            cosq = E2(nc.sbuf_tensor("cosq_s", [128, 1024], BF16))
            sinq = E2(nc.sbuf_tensor("sinq_s", [128, 1024], BF16))
            maskt = E2(nc.sbuf_tensor("maskt_s", [128, 8 * 512], BF16))
            pswp = E2(nc.sbuf_tensor("pswp_s", [128, 128], BF16))
            P.op("gpsimd", ld(cosk[:, :], d_cosk), incs=[(gq, 16)])
            P.op("gpsimd", ld(sink[:, :], d_sink), incs=[(gq, 16)])
            P.op("gpsimd", ld(cosq[:, :], d_cosq), incs=[(gq, 16)])
            P.op("gpsimd", ld(sinq[:, :], d_sinq), incs=[(gq, 16)])
            P.op("gpsimd", ld(maskt.ap().rearrange("p (j n) -> p j n", j=8),
                              d_mask.rearrange("j p n -> p j n")), incs=[(gq, 16)])
            r_consts = P.op("gpsimd", ld(pswp[:, :], d_pswp), incs=[(gq, 16)])
            wkb = E2(nc.sbuf_tensor("wkb", [128, 2 * 2048], BF16))
            wvb = E2(nc.sbuf_tensor("wvb", [128, 16 * 256], BF16))
            xns = [E2(nc.sbuf_tensor(f"xns{i}", [128, 16 * 512], BF16)) for i in range(2)]
            Kf = E2(nc.sbuf_tensor("Kf", [128, 2 * 4096], BF16))
            Vt = E2(nc.sbuf_tensor("Vt", [128, 32 * 256], BF16))
            Qf = E2(nc.sbuf_tensor("Qf", [128, 2 * 1024], BF16))
            kpre = [E2(nc.sbuf_tensor(f"kpre{i}", [128, 512], BF16)) for i in range(2)]
            qpre = [E2(nc.sbuf_tensor(f"qpre{i}", [128, 1024], BF16)) for i in range(2)]
            t1b = E2(nc.sbuf_tensor("t1b", [128, 512], BF16))
            t2b = E2(nc.sbuf_tensor("t2b", [128, 512], BF16))
            with ExitStack() as p1s:
                fm_recs, r_p1_ev = emit_ln(
                    "a", d_xseq, list(range(NT)), 512,
                    lambda s: fm_scr[s].rearrange("c p n -> p c n"), p1s, [])
                P.flush(blk)
            with ExitStack() as p15s:
                own_recs, r_own_ev = emit_ln(
                    "b", d_xown, list(range(8)), 512,
                    lambda sx: own_scr.rearrange("c p n -> p c n")[
                        :, :, sx * 512:(sx + 1) * 512],
                    p15s, [r_p1_ev], tm_dst=own_tm)
                r_own_wr = own_recs[max(own_recs)]
                P.flush(blk)
            if debug:
                P.op("gpsimd", ld(dbg["xn"], own_scr), waits=[r_own_wr],
                     incs=[(gq, 16)])
                P.op("gpsimd", ld(dbg["fm0"], fm_scr[0]), waits=[fm_recs[0]],
                     incs=[(gq, 16)])

            scores = [E2(nc.sbuf_tensor(f"sc{i}", [128, 4096], F32)) for i in range(2)]
            probs = [E2(nc.sbuf_tensor(f"probs{i}", [128, 4096], BF16)) for i in range(2)]
            ptb = E2(nc.sbuf_tensor("ptb", [128, 2048], BF16))
            smax = E2(nc.sbuf_tensor("smax", [128, 8], F32))

            r_w_consumer = None
            r_xns_cons = [None, None]
            r_sc_free = [None, None]
            r_pS_free = [None] * 4
            r_probs_free = None
            r_pO_free = [None, None]
            r_oev_last = None
            r_ptev_last = None
            r_exp_prev = None
            r_prev_attn_done = r_own_ev
            r_dbg_hold_prev = []

            for g in range(NG):
                with ExitStack() as gps:
                    E3 = gps.enter_context
                    pK = [[E3(nc.psum_tensor(f"pK{g}{h}{i}", [128, 512], F32))
                           for i in range(2)] for h in range(2)]
                    pV = E3(nc.psum_tensor(f"pV{g}", [128, 1024], F32))
                    pW = [E3(nc.psum_tensor(f"pW{g}{i}", [128, 512], F32))
                          for i in range(2)]

                    wdeps = [r_w_consumer, r_consts]
                    r_wk = P.op("gpsimd", ld(
                        wkb.ap().rearrange("p (h n) -> p h n", h=2),
                        d_wk[2 * g:2 * g + 2].rearrange("h p n -> p h n")),
                        waits=wdeps, incs=[(gq, 16)])
                    r_wv = P.op("gpsimd", ld(
                        wvb.ap().rearrange("p (c n) -> p c n", c=16),
                        d_wv[:, g * 256:(g + 1) * 256].rearrange(
                            "(c p) n -> p c n", p=128)),
                        waits=wdeps, incs=[(gq, 16)])

                    r_rope = [None] * NSTR
                    r_kpre = [None] * NSTR
                    r_vev = [None] * NSTR
                    r_kmm = [None] * NSTR
                    for s in range(NSTR):
                        par = s % 2
                        r_xn = P.op("sync", ld(
                            xns[par].ap().rearrange("p (c n) -> p c n", c=16),
                            fm_scr[s].rearrange("c p n -> p c n")),
                            waits=[fm_recs[s], r_xns_cons[par]], incs=[(sq, 16)])

                        def f_kmm(e, par=par, g=g):
                            ins = None
                            for c in range(16):
                                for h in range(2):
                                    ins = e.matmul(
                                        pK[h][par][:, :],
                                        wkb[:, h * 2048 + c * 128: h * 2048 + (c + 1) * 128],
                                        xns[par][:, c * 512:(c + 1) * 512],
                                        start=(c == 0), stop=(c == 15))
                            return ins
                        extra = ([r_prev_attn_done, r_oev_last, r_ptev_last]
                                 + r_dbg_hold_prev if s == 0 else [])
                        rkmm = P.op("tensor", f_kmm,
                                    waits=[r_xn, r_wk,
                                           r_kpre[s - 2] if s >= 2 else None] + extra,
                                    incs=[(vt, 1)])
                        r_kmm[s] = rkmm

                        def f_vmm(e, par=par, g=g):
                            ins = None
                            for sub in range(4):
                                for c in range(16):
                                    ins = e.matmul(
                                        pV[:, sub * 256:(sub + 1) * 256],
                                        xns[par][:, c * 512 + sub * 128: c * 512 + (sub + 1) * 128],
                                        wvb[:, c * 256:(c + 1) * 256],
                                        start=(c == 0), stop=(c == 15))
                            return ins
                        rvmm = P.op("tensor", f_vmm,
                                    waits=[r_xn, r_wv,
                                           r_vev[s - 1] if s >= 1 else None],
                                    incs=[(vt, 1)])
                        r_xns_cons[par] = rvmm

                        def f_kpre(e, par=par, g=g):
                            e.tensor_copy(kpre[0][:, :], pK[0][par][:, :])
                            return e.tensor_copy(kpre[1][:, :], pK[1][par][:, :])
                        rkp = P.op("vector", f_kpre, waits=[rkmm], incs=[(vv, 1)])
                        r_kpre[s] = rkp

                        def f_swp(e, g=g):
                            e.matmul(pW[0][:, :], pswp[:, :], kpre[0][:, :],
                                     start=True, stop=True)
                            return e.matmul(pW[1][:, :], pswp[:, :], kpre[1][:, :],
                                            start=True, stop=True)
                        rsw = P.op("tensor", f_swp,
                                   waits=[rkp, r_consts,
                                          r_rope[s - 1] if s >= 1 else None],
                                   incs=[(vt, 1)])

                        rro = [rsw, r_consts]
                        for h in range(2):
                            def f_rope1(e, g=g, s=s, h=h):
                                e.tensor_mul(t1b[:, :],
                                             cosk[:, s * 512:(s + 1) * 512],
                                             kpre[h][:, :])
                                return e.tensor_mul(
                                    t2b[:, :], sink[:, s * 512:(s + 1) * 512],
                                    pW[h][:, :])
                            ra = P.op("vector", f_rope1, waits=rro, incs=[(vv, 1)])

                            def f_rope2(e, g=g, s=s, h=h):
                                return e.tensor_add(
                                    Kf[:, h * 4096 + s * 512: h * 4096 + (s + 1) * 512],
                                    t1b[:, :], t2b[:, :])
                            rro = [P.op("vector", f_rope2, waits=[ra], incs=[(vv, 1)])]
                        r_rope[s] = rro[0]

                        def f_vev(e, s=s, g=g):
                            return e.tensor_copy(
                                Vt[:, (4 * s) * 256:(4 * s + 4) * 256], pV[:, :])
                        r_vev[s] = P.op("vector", f_vev, waits=[rvmm], incs=[(vv, 1)])

                    r_wq = P.op("gpsimd", ld(
                        wkb.ap().rearrange("p (h n) -> p h n", h=2),
                        d_wq[2 * g:2 * g + 2].rearrange("h p n -> p h n")),
                        waits=[r_kmm[NSTR - 1]], incs=[(gq, 16)])
                    r_qhalf = [None, None]
                    for hf in range(2):
                        r_qhalf[hf] = P.op("sync", ld(
                            xns[hf].ap().rearrange("p (c n) -> p c n", c=8),
                            own_scr[8 * hf:8 * hf + 8].rearrange("c p n -> p c n")),
                            waits=[r_own_wr, r_xns_cons[hf]], incs=[(sq, 16)])

                    def f_qmm(e, g=g):
                        ins = None
                        for hf in range(2):
                            for c8 in range(8):
                                c = 8 * hf + c8
                                for h in range(2):
                                    for th in range(2):
                                        ins = e.matmul(
                                            pK[h][th][:, :],
                                            wkb[:, h * 2048 + c * 128: h * 2048 + (c + 1) * 128],
                                            xns[hf][:, c8 * 1024 + th * 512: c8 * 1024 + (th + 1) * 512],
                                            start=(c == 0), stop=(c == 15))
                        return ins
                    rqmm = P.op("tensor", f_qmm,
                                waits=[r_qhalf[0], r_qhalf[1], r_wq,
                                       r_kpre[NSTR - 1]],
                                incs=[(vt, 1)])
                    r_xns_cons = [rqmm, rqmm]
                    r_w_consumer = rqmm

                    def f_qpre(e, g=g):
                        ins = None
                        for h in range(2):
                            for th in range(2):
                                ins = e.tensor_copy(
                                    qpre[h][:, th * 512:(th + 1) * 512],
                                    pK[h][th][:, :])
                        return ins
                    rqp = P.op("vector", f_qpre, waits=[rqmm], incs=[(vv, 1)])

                    r_qrope = None
                    for h in range(2):
                        def f_qsw(e, h=h, g=g):
                            e.matmul(pW[0][:, :], pswp[:, :], qpre[h][:, 0:512],
                                     start=True, stop=True)
                            return e.matmul(pW[1][:, :], pswp[:, :],
                                            qpre[h][:, 512:1024],
                                            start=True, stop=True)
                        rqs = P.op("tensor", f_qsw,
                                   waits=[rqp, r_rope[NSTR - 1], r_qrope],
                                   incs=[(vt, 1)])

                        rqro = [rqs]
                        for th in range(2):
                            def f_qro1(e, h=h, g=g, th=th):
                                e.tensor_mul(t1b[:, :],
                                             cosq[:, th * 512:(th + 1) * 512],
                                             qpre[h][:, th * 512:(th + 1) * 512])
                                return e.tensor_mul(
                                    t2b[:, :], sinq[:, th * 512:(th + 1) * 512],
                                    pW[th][:, :])
                            ra = P.op("vector", f_qro1, waits=rqro, incs=[(vv, 1)])

                            def f_qro2(e, h=h, g=g, th=th):
                                return e.tensor_add(
                                    Qf[:, h * 1024 + th * 512: h * 1024 + (th + 1) * 512],
                                    t1b[:, :], t2b[:, :])
                            rqro = [P.op("vector", f_qro2, waits=[ra], incs=[(vv, 1)])]
                        r_qrope = rqro[0]

                if debug and g == 7:
                    rd1 = P.op("gpsimd", ld(dbg["kf"], Kf[:, :]),
                               waits=[r_rope[NSTR - 1]], incs=[(gq, 16)])
                    rd2 = P.op("gpsimd", ld(dbg["vt"], Vt[:, :]),
                               waits=[r_vev[NSTR - 1]], incs=[(gq, 16)])
                    rd3 = P.op("gpsimd", ld(dbg["qf"], Qf[:, :]),
                               waits=[r_qrope], incs=[(gq, 16)])
                    r_qrope = rd3
                    r_dbg_hold = [rd1, rd2]
                else:
                    r_dbg_hold = []

                with ExitStack() as ats:
                    E4 = ats.enter_context
                    pS = [E4(nc.psum_tensor(f"pS{g}{i}", [128, 512], F32))
                          for i in range(4)]
                    pTt = [E4(nc.psum_tensor(f"pTt{g}{i}", [128, 1024], BF16))
                           for i in range(2)]
                    pO = [E4(nc.psum_tensor(f"pO{g}{i}", [128, 512], F32))
                          for i in range(2)]

                    first_w = [r_qrope]
                    it_list = [(h, j) for h in range(2) for j in range(8)]
                    stt_ = {"rgm": {}, "rno": {},
                            "probs_free": [r_probs_free, r_probs_free]}

                    def emit_S(i):
                        nonlocal first_w
                        h, j = it_list[i]
                        Kj = K_SLOT[j]
                        nst = Kj // 512
                        scp = (h * 8 + j) % 2
                        r_scev = [None] * nst
                        for ks in range(nst):
                            def f_smm(e, h=h, j=j, ks=ks, g=g):
                                return e.matmul(
                                    pS[ks % 4][:, :],
                                    Qf[:, h * 1024 + j * 128: h * 1024 + (j + 1) * 128],
                                    Kf[:, h * 4096 + ks * 512: h * 4096 + (ks + 1) * 512],
                                    start=True, stop=True)
                            rsm = P.op("tensor", f_smm,
                                       waits=first_w + [
                                           r_scev[ks - 4] if ks >= 4 else
                                           r_pS_free[ks % 4]],
                                       incs=[(vt, 1)])
                            first_w = []

                            def f_sev(e, scp=scp, ks=ks, g=g):
                                return e.tensor_copy(
                                    scores[scp][:, ks * 512:(ks + 1) * 512],
                                    pS[ks % 4][:, :])
                            r_scev[ks] = P.op("vector", f_sev,
                                              waits=[rsm, r_sc_free[scp],
                                                     r_own_ev if g == 0 else None],
                                              incs=[(vv, 1)])
                            r_pS_free[ks % 4] = r_scev[ks]

                        def f_mask(e, scp=scp, j=j, Kj=Kj):
                            return e.tensor_add(
                                scores[scp][:, Kj - 512:Kj],
                                scores[scp][:, Kj - 512:Kj],
                                maskt[:, j * 512:(j + 1) * 512])
                        rgmk = P.op("vector", f_mask,
                                    waits=[r_scev[nst - 1]], incs=[(vv, 1)])

                        def f_max(e, scp=scp, Kj=Kj, i=i):
                            return e.reduce_max(smax[:, 4 * (i % 2):4 * (i % 2) + 1],
                                                scores[scp][:, 0:Kj],
                                                axis=AX.X, negate=True)
                        stt_["rgm"][i] = P.op("vector", f_max,
                                              waits=[rgmk, r_exp_prev],
                                              incs=[(vv, 1)])

                    def emit_X(i):
                        nonlocal r_exp_prev
                        h, j = it_list[i]
                        Kj = K_SLOT[j]
                        scp = (h * 8 + j) % 2
                        ip = i % 2

                        def f_exp(e, scp=scp, Kj=Kj, ip=ip):
                            return e.activation(probs[ip][:, 0:Kj],
                                                scores[scp][:, 0:Kj], AF.Exp,
                                                bias=smax[:, 4 * ip:4 * ip + 1])
                        rex = P.op("scalar", f_exp,
                                   waits=[stt_["rgm"][i], stt_["probs_free"][ip]],
                                   incs=[(vs, 1)])
                        r_sc_free[scp] = rex
                        r_exp_prev = rex

                        def f_rsum(e, Kj=Kj, ip=ip):
                            return e.reduce_sum(smax[:, 4 * ip + 1:4 * ip + 2],
                                                probs[ip][:, 0:Kj], axis=AX.X)
                        rsum = P.op("vector", f_rsum, waits=[rex], incs=[(vv, 1)])

                        def f_rden(e, ip=ip):
                            return e.reciprocal(smax[:, 4 * ip + 2:4 * ip + 3],
                                                smax[:, 4 * ip + 1:4 * ip + 2])
                        rnoa = P.op("vector", f_rden, waits=[rsum], incs=[(vv, 1)])

                        def f_norm(e, Kj=Kj, ip=ip):
                            return e.tensor_scalar(probs[ip][:, 0:Kj],
                                                   probs[ip][:, 0:Kj],
                                                   smax[:, 4 * ip + 2:4 * ip + 3],
                                                   None, ALU.mult)
                        stt_["rno"][i] = P.op("vector", f_norm, waits=[rnoa],
                                              incs=[(vv, 1)])

                    def emit_T(i):
                        nonlocal r_oev_last, r_ptev_last
                        h, j = it_list[i]
                        hg = 2 * g + h
                        Kj = K_SLOT[j]
                        nkt = Kj // 128
                        jpar = j % 2
                        ip = i % 2
                        rno = stt_["rno"][i]
                        nb = (nkt + 7) // 8
                        r_ptev = [None] * nb
                        r_avb = None
                        for b in range(nb):
                            kts = list(range(b * 8, min((b + 1) * 8, nkt)))

                            def f_tr(e, kts=kts, b=b, g=g, ip=ip):
                                ins = None
                                for kt in kts:
                                    ins = e.transpose(
                                        pTt[b % 2][:, (kt % 8) * 128:(kt % 8 + 1) * 128],
                                        probs[ip][:, kt * 128:(kt + 1) * 128],
                                        idn_b[:, :])
                                return ins
                            rtr = P.op("tensor", f_tr,
                                       waits=[rno,
                                              r_ptev[b - 2] if b >= 2 else None],
                                       incs=[(vt, 1)])

                            def f_ptev(e, kts=kts, b=b, g=g):
                                n = len(kts) * 128
                                return e.copy(
                                    ptb[:, (b % 2) * 1024:(b % 2) * 1024 + n],
                                    pTt[b % 2][:, 0:n])
                            r_ptev[b] = P.op("scalar", f_ptev, waits=[rtr],
                                             incs=[(vs, 1)])
                            r_ptev_last = r_ptev[b]

                            def f_av(e, kts=kts, b=b, h=h, jpar=jpar, nkt=nkt, g=g):
                                ins = None
                                for kt in kts:
                                    ins = e.matmul(
                                        pO[jpar][:, 0:128],
                                        Vt[:, kt * 256 + h * 128: kt * 256 + (h + 1) * 128],
                                        ptb[:, (b % 2) * 1024 + (kt % 8) * 128:
                                            (b % 2) * 1024 + (kt % 8 + 1) * 128],
                                        start=(kt == 0), stop=(kt == nkt - 1))
                                return ins
                            r_avb = P.op("tensor", f_av,
                                         waits=[r_ptev[b],
                                                r_pO_free[jpar] if b == 0 else None],
                                         incs=[(vt, 1)])
                        stt_["probs_free"][ip] = r_avb

                        def f_oev(e, hg=hg, j=j, jpar=jpar, g=g):
                            return e.tensor_copy(
                                heads[:, hg * 1024 + j * 128: hg * 1024 + (j + 1) * 128],
                                pO[jpar][:, 0:128])
                        roev = P.op("vector", f_oev, waits=[r_avb], incs=[(vv, 1)])
                        r_pO_free[jpar] = roev
                        r_oev_last = roev

                    for i in range(16):
                        emit_S(i)
                        if i >= 1:
                            emit_T(i - 1)
                        emit_X(i)
                    emit_T(15)
                    r_probs_free = stt_["probs_free"][0]
                    r_prev_attn_done = r_oev_last
                r_dbg_hold_prev = r_dbg_hold
                P.flush(blk)

            if debug:
                r_oev_last = P.op("gpsimd", ld(dbg["hd"], heads[:, :]),
                                  waits=[r_oev_last], incs=[(gq, 16)])
            P.flush(blk)

        with ExitStack() as accs:
            EA = accs.enter_context
            acc = EA(nc.sbuf_tensor("acc", [128, 16 * 1024], F32))
            dsti2 = EA(nc.sbuf_tensor("dsti2", [128, 8], mybir.dt.int32))
            with ExitStack() as p3s:
                E5 = p3s.enter_context
                wob = [E5(nc.sbuf_tensor(f"wob{i}", [128, 2048], BF16)) for i in range(2)]
                pAO = [E5(nc.psum_tensor(f"pAO{i}", [128, 512], F32)) for i in range(2)]
                r_ao = {}
                r_aoev = {}
                r_wo = [None, None]
                for m in range(16):
                    extra_w = [r_oev_last, r_ptev_last] if m == 0 else []
                    r_wo[m % 2] = P.op("gpsimd", ld(wob[m % 2][:, :], d_wo[m]),
                                       waits=[r_ao.get((m - 2, 1))] + extra_w,
                                       incs=[(gq, 16)])
                    for s2 in range(2):
                        def f_ao(e, m=m, s2=s2):
                            ins = None
                            for c in range(16):
                                ins = e.matmul(
                                    pAO[s2][:, :],
                                    wob[m % 2][:, c * 128:(c + 1) * 128],
                                    heads[:, c * 1024 + s2 * 512: c * 1024 + (s2 + 1) * 512],
                                    start=(c == 0), stop=(c == 15))
                            return ins
                        extra = ([r_oev_last, r_ptev_last] if m == 0 and s2 == 0
                                 else [])
                        r_ao[(m, s2)] = P.op("tensor", f_ao,
                                             waits=[r_wo[m % 2],
                                                    r_aoev.get((m - 1, s2))] + extra,
                                             incs=[(vt, 1)])

                        def f_aoev(e, m=m, s2=s2):
                            return e.tensor_copy(
                                acc[:, m * 1024 + s2 * 512: m * 1024 + (s2 + 1) * 512],
                                pAO[s2][:, :])
                        r_aoev[(m, s2)] = P.op("vector", f_aoev,
                                               waits=[r_ao[(m, s2)]], incs=[(vv, 1)])
                r_oproj_t = r_ao[(15, 1)]
                r_oproj_v = r_aoev[(15, 1)]
                if debug:
                    r_oproj_v = P.op("gpsimd", ld(dbg["ac1"], acc[:, :]),
                                     waits=[r_oproj_v], incs=[(gq, 16)])
                P.flush(blk)

            with ExitStack() as p4s:
                E6 = p4s.enter_context
                xnf = [E6(nc.sbuf_tensor(f"xnf{i}", [128, 1024], BF16)) for i in range(2)]
                gwb = [E6(nc.sbuf_tensor(f"gwb{i}", [128, 2048], BF16)) for i in range(2)]
                uwb = [E6(nc.sbuf_tensor(f"uwb{i}", [128, 2048], BF16)) for i in range(2)]
                dwb = [E6(nc.sbuf_tensor(f"dwb{i}", [128, 2048], BF16)) for i in range(2)]
                tsil = [E6(nc.sbuf_tensor(f"tsil{i}", [128, 512], F32)) for i in range(2)]
                rst = E6(nc.sbuf_tensor("rst", [128, 24], F32))
                dtm = E6(nc.sbuf_tensor("dtm", [128, 24], F32))
                em_b = E6(nc.sbuf_tensor("em_b", [128, 64], BF16))
                lacc = E6(nc.sbuf_tensor("lacc", [128, 64], F32))
                trilb = E6(nc.sbuf_tensor("trilb", [128, 128], BF16))
                evalsb = E6(nc.sbuf_tensor("evalsb", [128, 8], BF16))
                dsti = E6(nc.sbuf_tensor("dsti", [128, 8], mybir.dt.int32))
                routw = E6(nc.sbuf_tensor("routw_s", [128, 16 * 8], BF16))

                r_trl = P.op("gpsimd", ld(trilb[:, :], d_tril), incs=[(gq, 16)])
                r_evl = P.op("gpsimd", ld(evalsb[:, :], d_evals), incs=[(gq, 16)])
                r_rwl = None
                for c in range(16):
                    r_rwl = P.op("gpsimd", ld(routw[:, c * 8:(c + 1) * 8],
                                              d_rw[c * 128:(c + 1) * 128, :]),
                                 incs=[(gq, 16)])

                pQY = [E6(nc.psum_tensor(f"pQY{i}", [128, 512], F32)) for i in range(4)]
                pXT = [E6(nc.psum_tensor(f"pXT{i}", [128, 1024], BF16)) for i in range(2)]

                r_idx2 = None
                with ExitStack() as rts:
                    E7 = rts.enter_context
                    pL = E7(nc.psum_tensor("pL", [128, 512], F32))
                    r_rx = [None, None]
                    r_rmm = None
                    r_racc = None
                    r_rmm_hist = [None, None]
                    for c in range(16):
                        extra_x = [r_oproj_t] if c == 0 else []
                        r_rx[c % 2] = P.op("sync", ld(xnf[c % 2][:, :], own_scr[c]),
                                           waits=[r_own_wr, r_rmm_hist[c % 2]]
                                           + extra_x,
                                           incs=[(sq, 16)])

                        def f_rmm(e, c=c):
                            ins = None
                            for j in range(8):
                                ins = e.matmul(pL[:, j * 8:(j + 1) * 8],
                                               xnf[c % 2][:, j * 128:(j + 1) * 128],
                                               routw[:, c * 8:(c + 1) * 8],
                                               start=True, stop=True)
                            return ins
                        extra = ([r_oproj_t, r_oproj_v, r_rwl] if c == 0 else [])
                        r_rmm = P.op("tensor", f_rmm,
                                     waits=[r_rx[c % 2], r_racc] + extra,
                                     incs=[(vt, 1)])
                        r_rmm_hist[c % 2] = r_rmm

                        def f_racc(e, c=c):
                            if c == 0:
                                return e.tensor_copy(lacc[:, :], pL[:, 0:64])
                            return e.tensor_add(lacc[:, :], pL[:, 0:64], lacc[:, :])
                        r_racc = P.op("vector", f_racc, waits=[r_rmm, r_racc],
                                      incs=[(vv, 1)])

                    def f_mx(e):
                        ins = None
                        for j in range(8):
                            ins = e.reduce_max(rst[:, j:j + 1],
                                               lacc[:, j * 8:(j + 1) * 8], axis=AX.X)
                        return ins
                    r_mx = P.op("vector", f_mx, waits=[r_racc], incs=[(vv, 1)])

                    def f_sig(e):
                        return e.activation(rst[:, 8:16], rst[:, 0:8], AF.Sigmoid)
                    r_sg_ = P.op("scalar", f_sig, waits=[r_mx], incs=[(vs, 1)])

                    r_cast_last = None
                    for j in range(8):
                        def f_em(e, j=j):
                            return e.tensor_scalar(em_b[:, j * 8:(j + 1) * 8],
                                                   lacc[:, j * 8:(j + 1) * 8],
                                                   rst[:, j:j + 1], None,
                                                   ALU.is_equal)
                        r_em = P.op("vector", f_em, waits=[r_sg_], incs=[(vv, 1)])

                        def f_rk(e, j=j):
                            return e.matmul(pL[:, 64 + j * 8: 64 + (j + 1) * 8],
                                            trilb[:, :],
                                            em_b[:, j * 8:(j + 1) * 8],
                                            start=True, stop=True)
                        r_rk = P.op("tensor", f_rk, waits=[r_em, r_trl, r_racc],
                                    incs=[(vt, 1)])

                        def f_d1(e, j=j):
                            e.tensor_mul(dtm[:, 0:8], em_b[:, j * 8:(j + 1) * 8],
                                         pL[:, 64 + j * 8: 64 + (j + 1) * 8])
                            return e.tensor_mul(dtm[:, 8:16],
                                                em_b[:, j * 8:(j + 1) * 8],
                                                evalsb[:, :])
                        r_d1 = P.op("vector", f_d1, waits=[r_rk, r_evl, r_cast_last],
                                    incs=[(vv, 1)])

                        def f_d2(e):
                            return e.tensor_add(dtm[:, 16:24], dtm[:, 0:8],
                                                dtm[:, 8:16])
                        r_d2 = P.op("vector", f_d2, waits=[r_d1], incs=[(vv, 1)])

                        def f_d3(e):
                            e.reduce_sum(rst[:, 16:17], dtm[:, 16:24], axis=AX.X)
                            return e.reduce_sum(rst[:, 18:19], dtm[:, 0:8], axis=AX.X)
                        r_d3 = P.op("vector", f_d3, waits=[r_d2], incs=[(vv, 1)])

                        def f_d4(e, j=j):
                            e.tensor_scalar(rst[:, 17:18], rst[:, 16:17], 1.0,
                                            float(j * CAPT), ALU.mult, ALU.add)
                            return e.tensor_scalar(rst[:, 19:20], rst[:, 18:19],
                                                   float(CAPT) - 0.5, None,
                                                   ALU.is_gt)
                        r_d4 = P.op("vector", f_d4, waits=[r_d3], incs=[(vv, 1)])

                        def f_d5(e):
                            return e.scalar_tensor_tensor(rst[:, 20:21],
                                                          rst[:, 19:20], 1.0e6,
                                                          rst[:, 17:18],
                                                          ALU.mult, ALU.add)
                        r_d5 = P.op("vector", f_d5, waits=[r_d4], incs=[(vv, 1)])

                        def f_d6(e, j=j):
                            return e.tensor_copy(dsti[:, j:j + 1], rst[:, 20:21])
                        r_cast_last = P.op("vector", f_d6, waits=[r_d5],
                                           incs=[(vv, 1)])
                    P.flush(blk)

                r_idx_o = P.op("sync", ld(idx_scr, dsti[:, :]),
                               waits=[r_cast_last], incs=[(sq, 16)])
                r_idx2 = P.op("sync", ld(dsti2[:, :], idx_scr),
                              waits=[r_idx_o], incs=[(sq, 16)])

                r_z_wr_last = None
                with ExitStack() as exsc:
                    E9 = exsc.enter_context
                    stile = [E9(nc.sbuf_tensor(f"stile{i}", [128, 2048], BF16))
                             for i in range(2)]
                    xec = E9(nc.sbuf_tensor("xec", [128, 16 * CAPE], BF16))
                    ztile = [E9(nc.sbuf_tensor(f"ztile{i}", [128, 2048], BF16))
                             for i in range(2)]
                    zfmb = E9(nc.sbuf_tensor("zfmb", [128, 16 * CAPE], BF16))

                    r_scat = None
                    r_sts = [None, None]
                    for j in range(8):
                        rsl = P.op("sync", ld(stile[j % 2][:, :],
                                              own_tm[j * 128:(j + 1) * 128, :]),
                                   waits=[r_own_wr, r_sts[j % 2]], incs=[(sq, 16)])

                        def f_sc(e, j=j):
                            return e.tensor_scalar(stile[j % 2][:, :],
                                                   stile[j % 2][:, :],
                                                   rst[:, 8 + j:9 + j], None,
                                                   ALU.mult)
                        rsc = P.op("vector", f_sc, waits=[rsl, r_cast_last],
                                   incs=[(vv, 1)])

                        def f_scat(e, j=j):
                            return e.indirect_dma_start(
                                out=cpq_scr,
                                out_offset=IndirectOffsetOnAxis(dsti2[:, j:j + 1],
                                                                axis=0),
                                in_=stile[j % 2][:, :], in_offset=None,
                                bounds_check=8 * CAPE, oob_is_err=False)
                        r_scat = P.op("gpsimd", f_scat, waits=[rsc, r_idx2],
                                      incs=[(gq, 16)])
                        r_sts[j % 2] = r_scat

                    r_xec_ev = None
                    r_y = {}
                    r_h = {}
                    r_zev_prev = None
                    r_sil_prev = None
                    r_ylast = None
                    r_zmm_hist = [None, None]
                    for eidx in range(E):
                        r_tr2 = [None, None]
                        for tt in range(3):
                            rct = P.op("sync", ld(
                                stile[tt % 2][:, :],
                                cpq_scr[eidx * CAPE + tt * 128:
                                        eidx * CAPE + (tt + 1) * 128, :]),
                                waits=[r_scat, r_tr2[tt % 2],
                                       r_sts[tt % 2] if eidx == 0 else None],
                                incs=[(sq, 16)])
                            for half in range(2):
                                def f_ct(e, tt=tt, half=half):
                                    ins = None
                                    for c8 in range(8):
                                        c = half * 8 + c8
                                        ins = e.transpose(
                                            pXT[half][:, c8 * 128:(c8 + 1) * 128],
                                            stile[tt % 2][:, c * 128:(c + 1) * 128],
                                            idn_b[:, :])
                                    return ins
                                rtr = P.op("tensor", f_ct,
                                           waits=[rct, r_idn, r_xec_ev,
                                                  r_zev_prev if tt == 0 and half == 0 else None],
                                           incs=[(vt, 1)])
                                r_tr2[tt % 2] = rtr

                                def f_cte(e, tt=tt, half=half):
                                    dst = xec.ap().rearrange(
                                        "p (c n) -> p c n", c=16)[
                                        :, half * 8:(half + 1) * 8,
                                        tt * 128:(tt + 1) * 128]
                                    return e.tensor_copy(
                                        dst, pXT[half].ap().rearrange(
                                            "p (c n) -> p c n", c=8))
                                r_xec_ev = P.op("vector", f_cte, waits=[rtr],
                                                incs=[(vv, 1)])

                        rw1 = [None, None]
                        rw2 = [None, None]
                        for m in range(16):
                            wv_ = [r_y.get(m - 2) if m >= 2 else r_ylast]
                            rw1[m % 2] = P.op("gpsimd",
                                              ld(gwb[m % 2][:, :], d_g[eidx][m]),
                                              waits=wv_, incs=[(gq, 16)])
                            rw2[m % 2] = P.op("gpsimd",
                                              ld(uwb[m % 2][:, :], d_u[eidx][m]),
                                              waits=wv_, incs=[(gq, 16)])

                            def f_y(e, m=m):
                                ins = None
                                for c in range(16):
                                    e.matmul(pQY[m % 2][:, 0:CAPE],
                                             gwb[m % 2][:, c * 128:(c + 1) * 128],
                                             xec[:, c * CAPE:(c + 1) * CAPE],
                                             start=(c == 0), stop=(c == 15))
                                for c in range(16):
                                    ins = e.matmul(pQY[2 + m % 2][:, 0:CAPE],
                                                   uwb[m % 2][:, c * 128:(c + 1) * 128],
                                                   xec[:, c * CAPE:(c + 1) * CAPE],
                                                   start=(c == 0), stop=(c == 15))
                                return ins
                            r_y[m] = P.op(
                                "tensor", f_y,
                                waits=[rw1[m % 2], rw2[m % 2], r_xec_ev,
                                       r_h.get(m - 2), r_zev_prev],
                                incs=[(vt, 1)])

                            def f_sil(e, m=m):
                                return e.activation(tsil[m % 2][:, 0:CAPE],
                                                    pQY[m % 2][:, 0:CAPE], AF.Silu)
                            rsil = P.op("scalar", f_sil, waits=[r_y[m]],
                                        incs=[(vs, 1)])
                            r_sil_prev = rsil

                            def f_h(e, m=m):
                                return e.tensor_mul(
                                    heads[:, m * CAPE:(m + 1) * CAPE],
                                    tsil[m % 2][:, 0:CAPE],
                                    pQY[2 + m % 2][:, 0:CAPE])
                            r_h[m] = P.op("vector", f_h,
                                          waits=[rsil, r_oproj_t], incs=[(vv, 1)])

                        rwd = [None, None]
                        r_zfm = {}
                        r_zfm_ev = {}
                        for n in range(16):
                            rwd[n % 2] = P.op("gpsimd", ld(
                                dwb[n % 2][:, :], d_dn[eidx][n]),
                                waits=[r_zfm.get(n - 2), r_zev_prev],
                                incs=[(gq, 16)])

                            def f_zmm(e, n=n):
                                ins = None
                                for m in range(16):
                                    ins = e.matmul(
                                        pQY[n % 2][:, 0:CAPE],
                                        dwb[n % 2][:, m * 128:(m + 1) * 128],
                                        heads[:, m * CAPE:(m + 1) * CAPE],
                                        start=(m == 0), stop=(m == 15))
                                return ins
                            wz = [rwd[n % 2], r_zfm_ev.get(n - 2)]
                            if n == 0:
                                wz += [r_h.get(15), r_sil_prev]
                            r_zfm[n] = P.op("tensor", f_zmm, waits=wz,
                                            incs=[(vt, 1)])

                            def f_zfev(e, n=n):
                                return e.tensor_copy(
                                    zfmb[:, n * CAPE:(n + 1) * CAPE],
                                    pQY[n % 2][:, 0:CAPE])
                            r_zfm_ev[n] = P.op("vector", f_zfev,
                                               waits=[r_zfm[n]], incs=[(vv, 1)])

                        r_ztev = None
                        r_zwr = [None, None]
                        for tt in range(3):
                            for half in range(2):
                                def f_zt(e, tt=tt, half=half):
                                    ins = None
                                    for c8 in range(8):
                                        n = half * 8 + c8
                                        ins = e.transpose(
                                            pXT[half][:, c8 * 128:(c8 + 1) * 128],
                                            zfmb[:, n * CAPE + tt * 128:
                                                 n * CAPE + (tt + 1) * 128],
                                            idn_b[:, :])
                                    return ins
                                rzt2 = P.op("tensor", f_zt,
                                            waits=[r_zfm_ev.get(15), r_idn,
                                                   r_ztev], incs=[(vt, 1)])

                                def f_ztev(e, tt=tt, half=half):
                                    return e.tensor_copy(
                                        ztile[tt % 2][:, half * 1024:(half + 1) * 1024],
                                        pXT[half][:, :])
                                r_ztev = P.op("vector", f_ztev,
                                              waits=[rzt2, r_zwr[tt % 2]],
                                              incs=[(vv, 1)])

                            def f_zwr(e, eidx=eidx, tt=tt):
                                return e.dma_start(
                                    out=z_scr[eidx * CAPE + tt * 128:
                                              eidx * CAPE + (tt + 1) * 128, :],
                                    in_=ztile[tt % 2][:, :])
                            r_zwr[tt % 2] = P.op("sync", f_zwr, waits=[r_ztev],
                                                 incs=[(sq, 16)])
                        r_zev = r_ztev
                        r_zev_prev = r_zev
                        r_ylast = r_y[15]
                        r_z_wr_last = max((r for r in r_zwr if r is not None),
                                          key=lambda r: r.tok()[1])
                        r_zmm_hist = [r_zfm.get(14), r_zfm.get(15)]
                        P.flush(blk)

                with ExitStack() as shsc:
                    E10 = shsc.enter_context
                    xe = E10(nc.sbuf_tensor("xe_sh", [128, 16 * 1024], BF16))
                    r_xe = None
                    for c in range(16):
                        r_xe = P.op("sync",
                                    ld(xe[:, c * 1024:(c + 1) * 1024], own_scr[c]),
                                    waits=[r_own_wr,
                                           r_zev_prev if c == 0 else None],
                                    incs=[(sq, 16)])
                    r_yd = {}
                    r_hd = {}
                    r_zd = {}
                    rw1 = [None, None]
                    rw2 = [None, None]
                    for m in range(16):
                        wv_ = [r_yd.get((m - 2, 1)) if m >= 2 else r_ylast]
                        rw1[m % 2] = P.op("gpsimd", ld(gwb[m % 2][:, :], d_sg[m]),
                                          waits=wv_, incs=[(gq, 16)])
                        rw2[m % 2] = P.op("gpsimd", ld(uwb[m % 2][:, :], d_su[m]),
                                          waits=wv_, incs=[(gq, 16)])
                        for s2 in range(2):
                            def f_y(e, m=m, s2=s2):
                                ins = None
                                for c in range(16):
                                    e.matmul(pQY[s2][:, :],
                                             gwb[m % 2][:, c * 128:(c + 1) * 128],
                                             xe[:, c * 1024 + s2 * 512: c * 1024 + (s2 + 1) * 512],
                                             start=(c == 0), stop=(c == 15))
                                for c in range(16):
                                    ins = e.matmul(pQY[2 + s2][:, :],
                                                   uwb[m % 2][:, c * 128:(c + 1) * 128],
                                                   xe[:, c * 1024 + s2 * 512: c * 1024 + (s2 + 1) * 512],
                                                   start=(c == 0), stop=(c == 15))
                                return ins
                            extra = ([r_zev_prev, r_z_wr_last] if m == 0 and s2 == 0
                                     else [])
                            r_yd[(m, s2)] = P.op(
                                "tensor", f_y,
                                waits=[rw1[m % 2], rw2[m % 2], r_xe,
                                       r_hd.get((m - 1, s2))] + extra,
                                incs=[(vt, 1)])

                            def f_sil(e, s2=s2):
                                return e.activation(tsil[s2][:, :], pQY[s2][:, :],
                                                    AF.Silu)
                            rsil = P.op("scalar", f_sil, waits=[r_yd[(m, s2)]],
                                        incs=[(vs, 1)])

                            def f_h(e, m=m, s2=s2):
                                return e.tensor_mul(
                                    heads[:, m * 1024 + s2 * 512: m * 1024 + (s2 + 1) * 512],
                                    tsil[s2][:, :], pQY[2 + s2][:, :])
                            r_hd[(m, s2)] = P.op("vector", f_h,
                                                 waits=[rsil, r_oproj_t],
                                                 incs=[(vv, 1)])
                    rwd = [None, None]
                    for n in range(16):
                        rwd[n % 2] = P.op("gpsimd", ld(dwb[n % 2][:, :], d_sd[n]),
                                          waits=[r_zd.get((n - 2, 1)) if n >= 2
                                                 else r_zev_prev],
                                          incs=[(gq, 16)])
                        for s2 in range(2):
                            def f_z(e, n=n, s2=s2):
                                ins = None
                                for m in range(16):
                                    ins = e.matmul(
                                        pQY[s2][:, :],
                                        dwb[n % 2][:, m * 128:(m + 1) * 128],
                                        heads[:, m * 1024 + s2 * 512: m * 1024 + (s2 + 1) * 512],
                                        start=(m == 0), stop=(m == 15))
                                return ins
                            rzt = P.op("tensor", f_z,
                                       waits=[rwd[n % 2], r_hd[(15, 0)], r_hd[(15, 1)],
                                              r_zd.get((n - 1, s2))],
                                       incs=[(vt, 1)])

                            def f_zev(e, n=n, s2=s2):
                                sl = acc[:, n * 1024 + s2 * 512: n * 1024 + (s2 + 1) * 512]
                                return e.tensor_add(sl, pQY[s2][:, :], sl)
                            r_zd[(n, s2)] = P.op("vector", f_zev,
                                                 waits=[rzt, r_oproj_v],
                                                 incs=[(vv, 1)])
                    r_zlast = r_zd[(15, 1)]
                if debug:
                    r_zlast = P.op("gpsimd", ld(dbg["ac2"], acc[:, :]),
                                   waits=[r_zlast], incs=[(gq, 16)])
                P.flush(blk)

            with ExitStack() as p5s:
                E8 = p5s.enter_context
                xob = [E8(nc.sbuf_tensor(f"xob{i}", [128, 2048], F32)) for i in range(2)]
                otb = [E8(nc.sbuf_tensor(f"otb{i}", [128, 2048], F32)) for i in range(2)]
                mg = [E8(nc.sbuf_tensor(f"mg{i}", [128, 2048], BF16)) for i in range(2)]
                pOut = [E8(nc.psum_tensor(f"pOut{i}", [128, 2048], F32))
                        for i in range(2)]
                r_add = [None, None]
                r_store = [None] * 8
                for j in range(8):
                    par = j % 2
                    rxo = P.op("sync", ld(xob[par][:, :],
                                          d_xown[j * 128:(j + 1) * 128, :]),
                               waits=[r_add[par],
                                      r_zlast if j == 0 else None],
                               incs=[(sq, 16)])

                    def f_mg(e, j=j, par=par):
                        return e.indirect_dma_start(
                            out=mg[par][:, :], out_offset=None,
                            in_=z_scr,
                            in_offset=IndirectOffsetOnAxis(dsti2[:, j:j + 1],
                                                           axis=0),
                            bounds_check=8 * CAPE, oob_is_err=False)
                    rmg = P.op("gpsimd", f_mg,
                               waits=[r_z_wr_last, r_idx2, r_add[par]],
                               incs=[(gq, 16)])

                    def f_tro(e, j=j, par=par):
                        ins = None
                        for c in range(16):
                            ins = e.transpose(
                                pOut[par][:, c * 128:(c + 1) * 128],
                                acc[:, c * 1024 + j * 128: c * 1024 + (j + 1) * 128],
                                idn_f[:, :])
                        return ins
                    rto = P.op("tensor", f_tro,
                               waits=[r_zlast, r_idnf, r_add[par]], incs=[(vt, 1)])

                    def f_add(e, par=par):
                        return e.tensor_add(otb[par][:, :], pOut[par][:, :],
                                            xob[par][:, :])
                    ra1 = P.op("vector", f_add,
                               waits=[rto, rxo,
                                      r_store[j - 2] if j >= 2 else None],
                               incs=[(vv, 1)])

                    def f_add2(e, par=par):
                        return e.tensor_add(otb[par][:, :], otb[par][:, :],
                                            mg[par][:, :])
                    r_add[par] = P.op("vector", f_add2, waits=[ra1, rmg],
                                      incs=[(vv, 1)])

                    r_store[j] = P.op("sync", ld(d_out[j], otb[par][:, :]),
                                      waits=[r_add[par]], incs=[(sq, 16)])

                P.op("sync", lambda e: e.nop(), waits=[r_store[-1]])
                P.flush(blk)

    return nc


_NC_CACHE = None


def _get_nc():
    global _NC_CACHE
    if _NC_CACHE is None:
        _NC_CACHE = build_nc()
    return _NC_CACHE



def _perm_lhsT(A):
    return np.ascontiguousarray(
        A.reshape(16, 128, 16, 128).transpose(2, 1, 0, 3).reshape(16, 128, 2048))


def kernel(L0_attn_X, L0_attn_Wq, L0_attn_Wk, L0_attn_Wv, L0_attn_Wo,
           L0_ffn_router_w, L0_ffn_gate_w, L0_ffn_up_w, L0_ffn_down_w,
           L0_ffn_shared_gate_w, L0_ffn_shared_up_w, L0_ffn_shared_down_w):
    f32 = lambda a: np.ascontiguousarray(np.asarray(a, dtype=np.float32))
    X = f32(L0_attn_X)

    wq_p = _perm_lhsT(f32(L0_attn_Wq).T)
    wk_p = _perm_lhsT(f32(L0_attn_Wk).T)
    wv_n = np.ascontiguousarray(f32(L0_attn_Wv).T)
    wo_p = _perm_lhsT(f32(L0_attn_Wo).T)
    rw_n = f32(L0_ffn_router_w)
    gate_p = np.stack([_perm_lhsT(f32(L0_ffn_gate_w)[e]) for e in range(E)])
    up_p = np.stack([_perm_lhsT(f32(L0_ffn_up_w)[e]) for e in range(E)])
    down_p = np.stack([_perm_lhsT(f32(L0_ffn_down_w)[e]) for e in range(E)])
    tril_t = np.ascontiguousarray(np.tril(np.ones((128, 128), np.float32), -1).T)
    evals_t = np.ascontiguousarray(
        np.tile((np.arange(8, dtype=np.float32) * CAPE)[None, :], (128, 1)))
    sg_p = _perm_lhsT(f32(L0_ffn_shared_gate_w))
    su_p = _perm_lhsT(f32(L0_ffn_shared_up_w))
    sd_p = _perm_lhsT(f32(L0_ffn_shared_down_w))

    theta = (10000.0 ** (-2.0 * np.arange(DH // 2, dtype=np.float32) / DH)
             ).astype(np.float32)
    pos = np.arange(T, dtype=np.float32)
    ang = theta[:, None] * pos[None, :]
    cos64 = np.cos(ang).astype(np.float32)
    sin64 = np.sin(ang).astype(np.float32)
    cosk = np.repeat(cos64, 2, axis=0)
    sink = np.empty_like(cosk)
    sink[0::2] = -sin64
    sink[1::2] = sin64
    scale = np.float32(DH ** -0.5)

    pswp = np.zeros((128, 128), np.float32)
    for i in range(64):
        pswp[2 * i + 1, 2 * i] = 1.0
        pswp[2 * i, 2 * i + 1] = 1.0

    sel8 = np.zeros((8, 1024), np.float32)
    for e in range(8):
        sel8[e, e * 128:(e + 1) * 128] = 1.0

    nc = _get_nc()
    in_maps = []
    for core in range(8):
        b, q = core // 4, core % 4
        tiles = own_tiles_for(q)
        xown = np.ascontiguousarray(
            np.concatenate([X[b, t * 128:(t + 1) * 128] for t in tiles], axis=0))
        qpos = np.concatenate([t * 128 + np.arange(128) for t in tiles])
        cosq = np.ascontiguousarray(cosk[:, qpos] * scale)
        sinq = np.ascontiguousarray(sink[:, qpos] * scale)
        maskt = np.zeros((8, 128, 512), np.float32)
        for j, t in enumerate(tiles):
            Kj = K_SLOT[j]
            kcols = Kj - 512 + np.arange(512)
            qrows = t * 128 + np.arange(128)
            maskt[j] = np.where(kcols[None, :] > qrows[:, None], -30000.0, 0.0)
        in_maps.append(dict(
            xseq=np.ascontiguousarray(X[b]), xown=xown,
            cosk=np.ascontiguousarray(cosk), sink=np.ascontiguousarray(sink),
            cosq=cosq, sinq=sinq, maskt=maskt, pswp=pswp, sel8=sel8,
            wq_p=wq_p, wk_p=wk_p, wv_n=wv_n, wo_p=wo_p, rw_n=rw_n,
            gate_p=gate_p, up_p=up_p, down_p=down_p,
            tril_t=tril_t, evals_t=evals_t,
            sg_p=sg_p, su_p=su_p, sd_p=sd_p))

    res = run_bass_kernel_spmd(nc, in_maps, core_ids=list(range(8)))

    out = np.empty((B, T, D), np.float32)
    for core in range(8):
        b, q = core // 4, core % 4
        tiles = own_tiles_for(q)
        oo = res.results[core]["out_o"]
        for j, t in enumerate(tiles):
            out[b, t * 128:(t + 1) * 128, :] = oo[j]
    return out


